# version 08
# baseline (speedup 1.0000x reference)
import math
import numpy as np
from contextlib import ExitStack

import concourse.bass as bass
import concourse.mybir as mybir
from concourse.bass_utils import run_bass_kernel_spmd

F32 = mybir.dt.float32
BF16 = mybir.dt.bfloat16
AF = mybir.ActivationFunctionType
ALU = mybir.AluOpType
AX = mybir.AxisListType

D = 1024
NDC = 8
HEADS = 16
HD = 64
NG = 8
DL = 256
NE = 8
IMOE = 4096
IMLP = 2048
LAM = 0.01
EPS = 1e-5
INV32 = 1.0 / 32.0
BIG = 1.0e30

VEC_COLS = {
    "lnq_s": 0, "lnq_b": 8, "lnk_s": 16, "lnk_b": 24,
    "lnv_s": 32, "lnv_b": 40, "lnm_s": 48, "lnm_b": 56,
    "bq": 64, "bk": 72, "bv": 80, "bo": 88,
    "b1": 96, "b2": 112, "gate": 120,
}
NVEC = 192
BA = {"bq": 0, "bk": 8, "bv": 16, "bo": 24, "b1": 32}

RW_RCP = 0
RW_GT = 64
RW_TMPG = 384
RW_BCA = 512
RW_I128C = 512
RW_I128R = 640
RW_I512 = 512
RW_QC = 0
RW_KC = 16
RW_MC = 32
RW_MT = 48


class Prog:
    ENGS = ("sync", "gpsimd", "tensor", "vector", "scalar")

    def __init__(self, nc, es):
        self.nc = nc
        self.es = es
        self.q = {e: [] for e in self.ENGS}
        self.sems = {}
        self.cnt = {}
        for name in ("dG", "dS", "PE", "ACT", "DVE", "POOL"):
            self.sem(name)

    def sem(self, name):
        if name not in self.sems:
            self.sems[name] = self.es.enter_context(self.nc.semaphore(name))
            self.cnt[name] = 0
        return self.sems[name]

    def op(self, eng, meth, *args, sem=None, k=1, **kw):
        s = self.sem(sem) if sem else None

        def ap(x):
            if isinstance(x, (bass.SBTensorHandle, bass.PSumTensorHandle,
                              bass.DRamTensorHandle)):
                return x[:, :] if len(x.shape) == 2 else x[:]
            return x
        args = tuple(ap(x) for x in args)
        kw = {kk: ap(vv) for kk, vv in kw.items()}

        def f(e, meth=meth, args=args, kw=kw, s=s, k=k):
            r = getattr(e, meth)(*args, **kw)
            if s is not None:
                r.then_inc(s, k)
        self.q[eng].append(f)
        if sem:
            self.cnt[sem] += k
            return self.cnt[sem]
        return None

    def wait(self, eng, sem, val):
        if not val or val <= 0:
            return
        s = self.sem(sem)
        self.q[eng].append(lambda e, s=s, val=val: e.wait_ge(s, val))

    def waitd(self, eng, dep):
        if dep:
            self.wait(eng, dep[0], dep[1])

    def drain(self, eng):
        self.q[eng].append(lambda e: e.drain())

    MILE = {"tensor": "PE", "vector": "DVE", "scalar": "ACT",
            "gpsimd": "POOL"}

    def fence(self, eng):
        name = self.MILE[eng]
        self.drain(eng)
        s = self.sems[name]
        self.q[eng].append(lambda e, s=s: e.sem_inc(s, 1))
        self.cnt[name] += 1
        return self.cnt[name]

    BAR = {"tensor": "barT", "vector": "barV", "scalar": "barS",
           "gpsimd": "barG"}

    def barrier(self):
        for eng, bs in self.BAR.items():
            if eng == "gpsimd":
                continue
            self.sem(bs)
            self.drain(eng)
            s = self.sems[bs]
            self.q[eng].append(lambda e, s=s: e.sem_inc(s, 1))
            self.cnt[bs] += 1
        snap = [(n, c) for n, c in self.cnt.items()
                if c > 0 and not n.startswith(("tci", "tco"))]
        for eng in self.ENGS:
            for n, c in snap:
                self.wait(eng, n, c)

    def flush(self):
        with self.nc.Block() as blk:
            for eng in self.ENGS:
                if not self.q[eng]:
                    continue
                ops = self.q[eng]

                def run(e, ops=ops):
                    for f in ops:
                        f(e)
                getattr(blk, eng)(run)
        self.q = {e: [] for e in self.ENGS}


class DB:

    def __init__(self, n=2):
        self.n = n
        self.free = [None] * n

    def slot(self, i):
        return i % self.n

    def reuse_dep(self, i):
        return self.free[i % self.n]

    def set_done(self, i, dep):
        self.free[i % self.n] = dep


class WS:
    pass


class MK:
    def __init__(self, T=4096, stop_after=None, debug_taps=()):
        self.T = T
        self.NT8 = T // 512
        self.NTT = T // 128
        self.NW = T // 512
        self.NKC = T // 128
        self.stop_after = stop_after
        self.debug_taps = set(debug_taps)
        self.debug_out = {}
        self.nc = bass.Bass("TRN2", target_bir_lowering=False)
        self.es = ExitStack()
        self.build()

    def sb(self, name, shape, dtype, es=None):
        return (es or self.es).enter_context(self.nc.sbuf_tensor(name, shape, dtype))

    def bank(self, b, w=512, off=0, p=128, p0=0):
        return self.P8[p0:p0 + p, b * 512 + off: b * 512 + off + w]

    def vcol(self, name, i=0, n=1):
        c = VEC_COLS[name] + i
        return self.vecs[:, c:c + n]

    def bcol(self, name, i=0, n=1):
        c = BA[name] + i
        return self.biasA[:, c:c + n]

    def st(self, dc, c0=0, n=None):
        if n is None:
            n = self.T
        return self.stream[:, dc * self.T + c0: dc * self.T + c0 + n]

    def rw(self, c0, n, p0=0, p=1):
        return self.rows[p0:p0 + p, c0:c0 + n]

    def cw(self, c0, n, p=1):
        return self.cells[0:p, c0:c0 + n]

    def tap(self, name, ap):
        if name not in self.debug_taps:
            return
        p = self.p
        sh = list(ap.shape)
        t = self.nc.dram_tensor("dbg_" + name, sh, ap.dtype, kind="ExternalOutput")
        self.debug_out[name] = "dbg_" + name
        for s in ("DVE", "ACT", "PE", "POOL"):
            p.wait("sync", s, p.cnt[s])
        p.op("sync", "dma_start", t[:, :] if len(sh) == 2 else t[:], ap,
             sem="dS", k=16)
        p.wait("sync", "dS", p.cnt["dS"])
        s = p.sem("TAP")
        p.q["sync"].append(lambda e, s=s: e.sem_inc(s, 1))
        p.cnt["TAP"] += 1
        for e in ("tensor", "vector", "scalar", "gpsimd"):
            p.wait(e, "TAP", p.cnt["TAP"])

    def bcast_cols(self, src, ncols, b, tb=7):
        p = self.p
        for t in range(ncols):
            m = p.op("tensor", "transpose", self.bank(tb, 128, p=1),
                     src[:, t:t + 1], self.ident_f[:, :], sem="PE")
            p.wait("vector", "PE", m)
            p.drain("vector")
            v = p.op("vector", "tensor_copy", self.rw(RW_BCA + t * 128, 128),
                     self.bank(tb, 128, p=1), sem="DVE")
            p.wait("tensor", "DVE", v)
            p.op("tensor", "matmul", self.bank(b, 128, off=t * 128),
                 self.ones_f[0:1, :], self.rw(RW_BCA + t * 128, 128),
                 start=True, stop=True, sem="PE")
        return p.cnt["PE"]

    def build(self):
        nc, es = self.nc, self.es
        T = self.T

        self.x_h = nc.dram_tensor("x", [D, T], F32, kind="ExternalInput")
        self.out_h = nc.dram_tensor("out", [D, T], F32, kind="ExternalOutput")
        self.vecs_h = nc.dram_tensor("vecsin", [128, NVEC], F32, kind="ExternalInput")
        wnames = {
            "aWq": [D, D], "aWk": [D, D], "aWv": [D, D], "aWo": [D, D],
            "aW1": [D, IMLP], "aW2": [IMLP, D],
            "bWq": [D, D], "bWk": [D, D], "bWv": [D, D], "bWo": [D, D],
            "bWdown": [D, DL], "bWupK": [DL, D], "bWupV": [DL, D],
            "wgT": [NE, D, IMOE], "wuT": [NE, D, IMOE], "wdT": [NE, IMOE, D],
        }
        self.w_h = {k: nc.dram_tensor(k, s, F32, kind="ExternalInput")
                    for k, s in wnames.items()}
        self.attscr = nc.dram_tensor("attscr", [D, T], BF16)
        self.wgB = nc.dram_tensor("wgB", [NE, D, IMOE], BF16)
        self.wuB = nc.dram_tensor("wuB", [NE, D, IMOE], BF16)
        self.wdB = nc.dram_tensor("wdB", [NE, IMOE, D], BF16)

        self.stream = self.sb("stream", [128, NDC * T], F32)
        self.vecs = self.sb("vecs", [128, NVEC], F32)
        self.ones_f = self.sb("ones_f", [128, 128], F32)
        self.ones_b = self.sb("ones_b", [128, 8], BF16)
        self.ident_f = self.sb("ident_f", [128, 128], F32)
        self.csPool = self.sb("csPool", [128, 1024], BF16)
        self.rows = self.sb("rows", [128, 1024], F32)
        self.biasA = self.sb("biasA", [128, 48], F32)
        self.R_tok = self.sb("R_tok", [128, 32], F32)
        self.mu_tok = self.sb("mu_tok", [128, 32], F32)
        self.r1_tok = self.sb("r1_tok", [128, 32], F32)
        self.Mneg = self.sb("Mneg", [128, 16], F32)
        self.loc2 = self.sb("loc2", [128, 2], F32)
        self.cells = self.sb("cells", [128, 52], F32)
        self.maskc = self.sb("maskc", [128, 1], F32)
        self.vpm = self.sb("vpm", [128, 64], F32)
        self.constA = self.sb("constA", [128, 2], F32)
        self.P8 = es.enter_context(nc.psum_tensor("P8", [128, 4096], F32))

        self.p = Prog(nc, es)

        self.phase_load()
        self.phase_transcode()
        done = self.stop_after == "load"
        if not done:
            self.sub_attn_a()
            done = self.stop_after == "attn_a"
        self.p.barrier()
        for eng in Prog.ENGS:
            for dep in self.transcode_done:
                self.p.waitd(eng, dep)
        self.stg_es.close()
        if not done:
            self.sub_moe(0)
            done = self.stop_after == "moe1"
        if not done:
            self.sub_attn_b()
            done = self.stop_after == "attn_b"
        if not done:
            self.sub_moe(1)

        self.phase_out()
        self.p.flush()
        self.es.close()

    def phase_load(self):
        p, T = self.p, self.T
        for dc in range(NDC):
            p.op("sync", "dma_start", self.st(dc),
                 self.x_h[dc * 128:(dc + 1) * 128, :], sem="dS", k=16)
        dv = p.op("sync", "dma_start", self.vecs[:, :], self.vecs_h[:, :],
                  sem="dS", k=16)
        p.op("vector", "memset", self.ones_f[:, :], 1.0)
        p.op("vector", "memset", self.ones_b[:, :], 1.0)
        p.op("vector", "memset", self.constA[:, 0:1], EPS)
        p.op("vector", "memset", self.loc2[:, :], 0.0)
        p.op("vector", "drain")
        p.op("vector", "memset", self.loc2[0:64, 0:1], 1.0)
        p.op("vector", "drain")
        p.op("vector", "memset", self.loc2[64:128, 1:2], 1.0)
        p.op("vector", "drain")
        p.op("vector", "memset", self.constA[:, 1:2], 1e-38, sem="DVE")
        self.const_dep = p.cnt["DVE"]
        self.gate_dep = 0
        p.op("gpsimd", "iota", self.rw(RW_I128C, 128, p=128), [[1, 128]],
             channel_multiplier=0, allow_small_or_imprecise_dtypes=True)
        p.drain("gpsimd")
        p.op("gpsimd", "iota", self.rw(RW_I128R, 128, p=128), [[0, 128]],
             channel_multiplier=1, allow_small_or_imprecise_dtypes=True, sem="POOL")
        g1 = p.cnt["POOL"]
        p.wait("vector", "POOL", g1)
        p.drain("vector")
        p.op("vector", "tensor_tensor", self.ident_f[:, :],
             self.rw(RW_I128C, 128, p=128), self.rw(RW_I128R, 128, p=128),
             op=ALU.is_equal)
        p.drain("vector")
        p.op("vector", "tensor_scalar", self.maskc[:, :],
             self.rw(RW_I128R, 1, p=128), 127.0, -999.0,
             op0=ALU.is_equal, op1=ALU.mult, sem="DVE")
        self.ident_dep = p.cnt["DVE"]
        p.wait("gpsimd", "DVE", self.ident_dep)
        p.drain("gpsimd")
        p.op("gpsimd", "iota", self.rw(RW_BCA, 32, p=128), [[128, 32]],
             channel_multiplier=1, allow_small_or_imprecise_dtypes=True, sem="POOL")
        g2 = p.cnt["POOL"]
        p.wait("scalar", "POOL", g2)
        p.drain("scalar")
        p.op("scalar", "activation", self.vpm[:, 0:32], self.rw(RW_BCA, 32, p=128),
             AF.Exp, scale=LAM)
        p.drain("scalar")
        p.op("scalar", "activation", self.vpm[:, 32:64], self.rw(RW_BCA, 32, p=128),
             AF.Exp, scale=-LAM, sem="ACT")
        a2 = p.cnt["ACT"]
        p.wait("vector", "ACT", a2)
        p.drain("vector")
        p.op("vector", "tensor_scalar", self.vpm[:, 0:64], self.vpm[:, 0:64],
             INV32, None, op0=ALU.mult, sem="DVE")
        self.vpm_dep = p.cnt["DVE"]
        self.load_dS = p.cnt["dS"]
        self.load_ACT = p.cnt["ACT"]

    def phase_transcode(self):
        p = self.p
        self.stg_es = ExitStack()
        stg = self.sb("stg", [128, 2048], BF16, self.stg_es)
        db = DB(2)
        jobs = []
        for e in range(NE):
            for src, dst in ((self.w_h["wgT"], self.wgB), (self.w_h["wuT"], self.wuB)):
                for dc in range(NDC):
                    for hh in range(4):
                        jobs.append((src[e, dc * 128:(dc + 1) * 128,
                                         hh * 1024:(hh + 1) * 1024],
                                     dst[e, dc * 128:(dc + 1) * 128,
                                         hh * 1024:(hh + 1) * 1024], None))
            for r in range(4 * NDC):
                s = self.w_h["wdT"][e, r * 128:(r + 1) * 128, :].rearrange(
                    "(r p) d -> p r d", p=128)
                d = self.wdB[e, r * 128:(r + 1) * 128, :].rearrange(
                    "(r p) d -> p r d", p=128)
                jobs.append((s, d, 1))
        for i, (s, d, r3) in enumerate(jobs):
            b = db.slot(i)
            tile = stg[:, b * 1024:(b + 1) * 1024]
            if r3:
                tile = tile.rearrange("p (r d) -> p r d", r=r3)
            b2 = "AB"[b]
            p.waitd("gpsimd", db.reuse_dep(i))
            gi = p.op("gpsimd", "dma_start", tile, s, sem="tci" + b2, k=16)
            p.wait("gpsimd", "tci" + b2, gi)
            c = p.op("gpsimd", "dma_start", d, tile, sem="tco" + b2, k=16)
            db.set_done(i, ("tco" + b2, c))
        self.transcode_done = [("tcoA", p.cnt["tcoA"]), ("tcoB", p.cnt["tcoB"])]

    def phase_out(self):
        p = self.p
        for s in ("DVE", "ACT", "PE", "POOL"):
            p.wait("sync", s, p.cnt[s])
        for dc in range(NDC):
            p.op("sync", "dma_start", self.out_h[dc * 128:(dc + 1) * 128, :],
                 self.st(dc), sem="dS", k=16)
        p.wait("sync", "dS", p.cnt["dS"])

    def ln_stats(self, ws, double, bro=True):
        p, NT8 = self.p, self.NT8
        if not hasattr(self, "tokstat_read"):
            self.tokstat_read = 0
        p.wait("tensor", "dS", self.load_dS)
        p.wait("tensor", "DVE", self.const_dep)
        p.wait("scalar", "dS", self.load_dS)
        p.wait("vector", "dS", self.load_dS)
        sq = ws.sq
        sc = ws.stat
        mus, var, rstd, tmp = (sc[:, i * 512:(i + 1) * 512] for i in range(4))
        evac_done = {}
        sqdb = DB(2)
        j = 0
        for t8 in range(NT8):
            bS = (t8 % 2) * 2
            bQ = bS + 1
            ts8 = slice(t8 * 512, (t8 + 1) * 512)
            if t8 >= 2:
                p.wait("tensor", "DVE", evac_done[t8 - 2])
            for dc in range(NDC):
                xs = self.st(dc, t8 * 512, 512)
                p.op("tensor", "matmul", self.bank(bS), self.ones_f[:, :], xs,
                     start=(dc == 0), stop=(dc == NDC - 1),
                     sem="PE" if dc == NDC - 1 else None)
                b = sqdb.slot(j)
                sqb = sq[:, b * 512:(b + 1) * 512]
                p.waitd("scalar", sqdb.reuse_dep(j))
                p.drain("scalar")
                a = p.op("scalar", "activation", sqb, xs, AF.Square, sem="ACT")
                p.wait("tensor", "ACT", a)
                m = p.op("tensor", "matmul", self.bank(bQ), self.ones_f[:, :], sqb,
                         start=(dc == 0), stop=(dc == NDC - 1), sem="PE")
                sqdb.set_done(j, ("PE", m))
                j += 1
            mm_done = p.cnt["PE"]
            p.wait("vector", "PE", mm_done)
            p.drain("vector")
            p.op("vector", "tensor_scalar", mus, self.bank(bS), 1.0 / D, None,
                 op0=ALU.mult)
            p.drain("vector")
            v = p.op("vector", "tensor_scalar", tmp, mus, -1.0, None,
                     op0=ALU.mult, sem="DVE")
            p.wait("scalar", "DVE", v)
            p.drain("scalar")
            if bro:
                p.op("scalar", "activation", ws.mu_bro[:, ts8], tmp, AF.Copy)
            p.drain("scalar")
            p.wait("tensor", "DVE", v)
            p.wait("tensor", "DVE", self.tokstat_read)
            for tt in range(4):
                m = p.op("tensor", "transpose", self.bank(6, 128, off=tt * 128),
                         tmp[:, tt * 128:(tt + 1) * 128],
                         self.ident_f[:, :], sem="PE" if tt == 3 else None)
            p.wait("vector", "PE", p.cnt["PE"])
            p.drain("vector")
            cpv = p.op("vector", "tensor_copy",
                       self.mu_tok[:, t8 * 4: t8 * 4 + 4],
                       self.bank(6).rearrange("p (c x) -> p c x", c=4)[:, :, 0:1]
                       .rearrange("p c one -> p (c one)"),
                       sem="DVE")
            self.tokstat_read = cpv
            p.wait("scalar", "PE", p.cnt["PE"])
            p.drain("scalar")
            a = p.op("scalar", "activation", tmp, mus, AF.Square, sem="ACT")
            p.wait("vector", "ACT", a)
            p.drain("vector")
            p.op("vector", "scalar_tensor_tensor", var, self.bank(bQ), 1.0 / D,
                 tmp, op0=ALU.mult, op1=ALU.subtract, sem="DVE")
            p.wait("scalar", "DVE", p.cnt["DVE"])
            p.drain("scalar")
            p.wait("scalar", "DVE", self.const_dep)
            p.op("scalar", "activation", tmp, var, AF.Ln, bias=self.constA[:, 0:1])
            p.drain("scalar")
            a = p.op("scalar", "activation", rstd, tmp, AF.Exp, scale=-0.5,
                     sem="ACT")
            if double:
                p.wait("vector", "ACT", a)
                p.drain("vector")
                p.op("vector", "tensor_tensor", tmp, var, rstd, op=ALU.mult)
                p.drain("vector")
                v = p.op("vector", "tensor_tensor", tmp, tmp, rstd, op=ALU.mult,
                         sem="DVE")
                p.wait("scalar", "DVE", v)
                p.drain("scalar")
                p.op("scalar", "activation", tmp, tmp, AF.Ln, bias=self.constA[:, 0:1])
                p.drain("scalar")
                a2 = p.op("scalar", "activation", tmp, tmp, AF.Exp, scale=-0.5,
                          sem="ACT")
                p.wait("vector", "ACT", a2)
                p.drain("vector")
                p.op("vector", "tensor_tensor", var, rstd, tmp, op=ALU.mult,
                     sem="DVE")
                Rsrc = var
                p.wait("tensor", "DVE", p.cnt["DVE"])
                for tt in range(4):
                    p.op("tensor", "transpose", self.bank(6, 128, off=tt * 128),
                         rstd[:, tt * 128:(tt + 1) * 128],
                         self.ident_f[:, :], sem="PE" if tt == 3 else None)
                p.wait("vector", "PE", p.cnt["PE"])
                p.drain("vector")
                cpv = p.op("vector", "tensor_copy",
                           self.r1_tok[:, t8 * 4: t8 * 4 + 4],
                           self.bank(6).rearrange("p (c x) -> p c x", c=4)[:, :, 0:1]
                           .rearrange("p c one -> p (c one)"),
                           sem="DVE")
                self.tokstat_read = cpv
            else:
                Rsrc = rstd
            p.wait("vector", "ACT", p.cnt["ACT"])
            p.drain("vector")
            if bro:
                p.op("vector", "tensor_copy", ws.R_bro[:, ts8], Rsrc)
            p.drain("vector")
            rdone = p.op("vector", "tensor_copy", Rsrc[:, 0:1], Rsrc[:, 0:1],
                         sem="DVE")
            p.wait("tensor", "DVE", rdone)
            p.wait("tensor", "DVE", self.tokstat_read)
            for tt in range(4):
                p.op("tensor", "transpose", self.bank(6, 128, off=tt * 128),
                     Rsrc[:, tt * 128:(tt + 1) * 128],
                     self.ident_f[:, :], sem="PE" if tt == 3 else None)
            p.wait("vector", "PE", p.cnt["PE"])
            p.drain("vector")
            cpv = p.op("vector", "tensor_copy",
                       self.R_tok[:, t8 * 4: t8 * 4 + 4],
                       self.bank(6).rearrange("p (c x) -> p c x", c=4)[:, :, 0:1]
                       .rearrange("p c one -> p (c one)"),
                       sem="DVE")
            self.tokstat_read = cpv
            evac_done[t8] = p.cnt["DVE"]
        p.drain("vector")
        p.drain("scalar")
        self.stats_done = p.cnt["DVE"]

    def prep_bias_cs(self, w_hbm, s_name, b_name, badd_name, bias_name, cs_slot,
                     scale, ws, want_bias=True, identity_s=False):
        p = self.p
        wfp, wca, rhs2, rhs2b, cscol = ws.wfp, ws.wca, ws.rhs2, ws.rhs2b, ws.cscol
        p.drain("vector")
        for dc in range(NDC):
            if identity_s:
                p.op("vector", "tensor_copy", rhs2[:, 2 * dc:2 * dc + 1],
                     self.ones_f[:, 0:1])
            else:
                p.op("vector", "tensor_copy", rhs2[:, 2 * dc:2 * dc + 1],
                     self.vecs[:, VEC_COLS[s_name] + dc:VEC_COLS[s_name] + dc + 1])
            p.op("vector", "tensor_copy", rhs2[:, 2 * dc + 1:2 * dc + 2],
                 self.vecs[:, VEC_COLS[b_name] + dc:VEC_COLS[b_name] + dc + 1]
                 if not identity_s else self.ones_f[:, 0:1])
        p.waitd("vector", getattr(self, "wca_free", None))
        p.drain("vector")
        rdy = p.op("vector", "tensor_copy", rhs2b[:, :], rhs2[:, :], sem="DVE")
        if not hasattr(self, "pb_wdb"):
            self.pb_wdb = DB(2)
        wdb = self.pb_wdb
        for dc in range(NDC):
            b = wdb.slot(dc)
            p.waitd("sync", wdb.reuse_dep(dc))
            d = p.op("sync", "dma_start", wfp[:, b * 1024:(b + 1) * 1024],
                     w_hbm[dc * 128:(dc + 1) * 128, :], sem="pb" + "AB"[b], k=16)
            p.wait("vector", "pb" + "AB"[b], d)
            p.drain("vector")
            p.waitd("vector", getattr(self, "wca_free", None))
            c = p.op("vector", "tensor_copy", wca[:, b * 1024:(b + 1) * 1024],
                     wfp[:, b * 1024:(b + 1) * 1024], sem="DVE")
            wdb.set_done(dc, ("DVE", c))
            p.wait("tensor", "DVE", max(c, rdy))
            for oc in range(NDC):
                m = p.op("tensor", "matmul", self.bank(oc, 2),
                         wca[:, b * 1024 + oc * 128: b * 1024 + oc * 128 + 128],
                         rhs2b[:, 2 * dc:2 * dc + 2],
                         start=(dc == 0), stop=(dc == NDC - 1),
                         sem="PE" if oc == NDC - 1 else None)
            if dc >= 1:
                self.wca_free = ("PE", p.fence("tensor"))
        done = p.fence("tensor")
        self.wca_free = ("PE", done)
        p.wait("vector", "PE", done)
        row, col0 = cs_slot
        for oc in range(NDC):
            if want_bias:
                p.drain("vector")
                p.op("vector", "tensor_scalar",
                     self.biasA[:, BA[bias_name] + oc:BA[bias_name] + oc + 1],
                     self.bank(oc, 1, off=1),
                     self.vecs[:, VEC_COLS[badd_name] + oc:VEC_COLS[badd_name] + oc + 1],
                     scale, op0=ALU.add, op1=ALU.mult, sem="DVE")
            p.drain("vector")
            v = p.op("vector", "tensor_copy", cscol[:, 0:1], self.bank(oc, 1),
                     sem="DVE")
            p.wait("tensor", "DVE", v)
            m = p.op("tensor", "transpose", self.bank(oc, 128, off=256, p=1),
                     cscol[:, 0:1], self.ident_f[:, :], sem="PE")
            p.wait("vector", "PE", m)
            p.drain("vector")
            p.op("vector", "tensor_scalar",
                 self.csPool[row:row + 1, col0 + oc * 128: col0 + oc * 128 + 128],
                 self.bank(oc, 128, off=256, p=1), scale, None,
                 op0=ALU.mult, sem="DVE")
        p.drain("vector")
        return p.cnt["DVE"]

    def sub_attn_a(self):
        p, T = self.p, self.T
        with ExitStack() as pes:
            ws = WS()
            ws.mu_bro = self.sb("a_mu", [128, T], BF16, pes)
            ws.R_bro = self.sb("a_R", [128, T], BF16, pes)
            with ExitStack() as ses:
                ws.sq = self.sb("a_sq", [128, 1024], F32, ses)
                ws.stat = self.sb("a_stat", [128, 2048], F32, ses)
                self.ln_stats(ws, double=True)
            self.p.barrier()
            with ExitStack() as ses:
                ws.wfp = self.sb("a_wfp", [128, 2048], F32, ses)
                ws.wca = self.sb("a_wca", [128, 2048], BF16, ses)
                ws.rhs2 = self.sb("a_rhs2", [128, 16], F32, ses)
                ws.rhs2b = self.sb("a_rhs2b", [128, 16], BF16, ses)
                ws.cscol = self.sb("a_cscol", [128, 1], F32, ses)
                self.prep_bias_cs(self.w_h["aWq"], "lnq_s", "lnq_b", "bq", "bq",
                                  (0, 0), INV32, ws)
                self.prep_bias_cs(self.w_h["aWk"], "lnk_s", "lnk_b", "bk", "bk",
                                  (32, 0), 1.0, ws)
                self.prep_bias_cs(self.w_h["aWv"], "lnv_s", "lnv_b", "bv", "bv",
                                  (64, 0), 1.0, ws)
            self.p.barrier()
            self.attn_groups(ws, variant="a")
            self.p.barrier()
            self.y0_add(ws)
            self.p.barrier()
            self.wo_phase(ws, self.w_h["aWo"], with_bias=True)
            self.p.barrier()
            with ExitStack() as ses:
                ws.sq = self.sb("a_sq2", [128, 1024], F32, ses)
                ws.stat = self.sb("a_stat2", [128, 2048], F32, ses)
                self.ln_stats(ws, double=False)
            self.p.barrier()
            with ExitStack() as ses:
                self.ffn_like(ws, ses, kind="mlp")
            self.p.barrier()

    def attn_groups(self, ws, variant):
        p, T, NT8, NW = self.p, self.T, self.NT8, self.NW
        a = variant == "a"
        V = variant
        with ExitStack() as ses:
            Qg = self.sb(V + "t_Qg", [128, T], BF16, ses)
            Kg = self.sb(V + "t_Kg", [128, T], BF16, ses)
            Vb = self.sb(V + "t_Vb", [128, 130 * self.NKC], BF16, ses)
            PT = self.sb(V + "t_PT", [128, max(self.NKC * 256, 7168)], BF16, ses)
            xbf = PT[:, 0:4096]
            qsc = self.sb(V + "t_qsc", [128, 512], F32, ses)
            atts = qsc[:, 0:512]
            attst = self.sb(V + "t_attst", [128, 1024], BF16, ses)
            sqn = attst[:, :].bitcast(F32)
            if a:
                wpg = PT[:, 4096:7168]
                wfpg = self.sb("at_wfpg", [128, 1024], F32, ses)
                self.pt_free = 0
                qtw = self.sb("at_qpad", [128, 1024], BF16, ses)
                qhw = Ew = wupg = None
            else:
                wpg = PT[:, 4096:6144]
                wfpg = None
                self.pt_free = 0
                qtw = self.sb("bt_qtw", [128, 1024], BF16, ses)
                qhw = self.sb("bt_qhw", [128, 1024], BF16, ses)
                Ew = None
                wupg = PT[:, 6144:6656]
                self.E0pb = self.sb("bt_E0p", [128, 512], BF16, ses)
                self.E0mb = self.sb("bt_E0m", [128, 512], BF16, ses)
                self.CM = self.sb("bt_CM", [128, 896], BF16, ses)
                self.build_decay_tiles()
                self.tap("CM", self.CM[:, :])
                self.tap("E0p", self.E0pb[:, :])
                self.tap("E0m", self.E0mb[:, :])
            for kc in range(self.NKC):
                p.op("vector", "memset", Vb[:, kc * 130 + 64: kc * 130 + 65], 1.0)
                p.op("vector", "memset", Vb[:, kc * 130 + 129: kc * 130 + 130], 1.0)
            self.vb_ones = p.fence("vector")
            self.nsqn = sqn
            self.quad_hist = []
            self.attb_free = None
            self.attst_dma = {}
            self.defer_norm = None
            self.attb_free = {}
            self.qw_dep = None
            self.qsc_act_free = 0
            self.qtw_pe = 0
            self.group_core_pe = 0
            self.group_core_dve = 0
            for g in range(NG):
                self.proj_group(ws, g, variant, Qg, Kg, Vb, xbf, qsc, sqn,
                                wpg, wfpg, wupg)
                mdep = self.finalize_M(g)
                for loc in range(2):
                    h = 2 * g + loc
                    for w in range(NW):
                        self.attn_core(ws, g, loc, h, w, variant, Qg, Kg,
                                       Vb, PT, atts, attst, qtw, qhw, Ew, mdep)
            p.drain("vector")
            p.drain("scalar")

    def stat_bcast_tq(self, ws, tq):
        p = self.p
        p.wait("tensor", "DVE", p.cnt["DVE"])
        p.wait("tensor", "DVE", self.stats_done)
        bc = self.bcast_cols(self.mu_tok[:, tq * 4:(tq + 1) * 4], 4, 6, tb=7)
        p.wait("vector", "PE", bc)
        p.drain("vector")
        mv = p.op("vector", "tensor_copy", ws.murow[:, :],
                  self.bank(6, 512, p=64), sem="DVE")
        p.wait("tensor", "DVE", mv)
        bc = self.bcast_cols(self.R_tok[:, tq * 4:(tq + 1) * 4], 4, 6, tb=7)
        p.wait("vector", "PE", bc)
        p.drain("vector")
        rv = p.op("vector", "tensor_copy", ws.Rb_sb[:, :], self.bank(6),
                  sem="DVE")
        return rv

    def build_decay_tiles(self):
        p = self.p
        p.barrier()
        cmstg = self.rows[0:128, 0:896]
        p.op("gpsimd", "iota", cmstg, [[1, 896]], base=-384,
             channel_multiplier=0, allow_small_or_imprecise_dtypes=True)
        p.drain("gpsimd")
        p.op("gpsimd", "iota", self.cw(50, 1, p=128), [[0, 1]],
             channel_multiplier=1, allow_small_or_imprecise_dtypes=True,
             sem="POOL")
        gd = p.cnt["POOL"]
        p.wait("vector", "POOL", gd)
        p.drain("vector")
        v = p.op("vector", "tensor_scalar", cmstg, cmstg,
                 self.cw(50, 1, p=128), 0.0,
                 op0=ALU.subtract, op1=ALU.min, sem="DVE")
        p.wait("scalar", "DVE", v)
        p.drain("scalar")
        a1 = p.op("scalar", "activation", self.CM[:, :], cmstg, AF.Exp,
                  scale=2.0 * LAM, sem="ACT")
        p.wait("gpsimd", "ACT", a1)
        p.drain("gpsimd")
        p.op("gpsimd", "iota", self.rw(RW_BCA, 512, p=128), [[1, 512]],
             channel_multiplier=0, allow_small_or_imprecise_dtypes=True,
             sem="POOL")
        g2 = p.cnt["POOL"]
        p.wait("scalar", "POOL", g2)
        p.drain("scalar")
        p.op("scalar", "activation", self.E0pb[:, :], self.rw(RW_BCA, 512, p=128),
             AF.Exp, scale=LAM)
        p.drain("scalar")
        ae = p.op("scalar", "activation", self.E0mb[:, :],
                  self.rw(RW_BCA, 512, p=128), AF.Exp, scale=-LAM, sem="ACT")
        self.decay_dep = (p.cnt["ACT"], v)
        p.wait("vector", "ACT", p.cnt["ACT"])
        p.drain("vector")

    def proj_group(self, ws, g, variant, Qg, Kg, Vb, xbf, qsc, sqn,
                   wpg, wfpg, wupg):
        p, T, NT8 = self.p, self.T, self.NT8
        a = variant == "a"
        if g > 0:
            p.wait("vector", "PE", self.group_core_pe)
            p.wait("vector", "DVE", self.group_core_dve)
            p.wait("scalar", "PE", self.group_core_pe)
            p.wait("tensor", "ACT", p.cnt["ACT"])
            p.wait("tensor", "DVE", self.group_core_dve)
            p.wait("sync", "PE", self.group_core_pe)
            p.wait("sync", "ACT", p.cnt["ACT"])
            p.wait("gpsimd", "PE", self.group_core_pe)
            p.wait("gpsimd", "ACT", p.cnt["ACT"])
            p.waitd("scalar", self.attst_dma.get(0))
            p.waitd("scalar", self.attst_dma.get(1))
        if a:
            specs = [(self.w_h["aWq"], 0, "lnq_s", INV32),
                     (self.w_h["aWk"], 1024, "lnk_s", 1.0),
                     (self.w_h["aWv"], 2048, "lnv_s", 1.0)]
            fold_dep = 0
            for wi, (wh, c0, s_name, scale) in enumerate(specs):
                d = p.op("sync", "dma_start",
                         wfpg[:, :].rearrange("p (dc c) -> p dc c", dc=NDC),
                         wh[:, g * 128:(g + 1) * 128]
                         .rearrange("(dc p) c -> p dc c", p=128), sem="wfa", k=16)
                p.wait("vector", "wfa", d)
                if g > 0 and wi == 0:
                    p.wait("vector", "PE", self.group_core_pe)
                p.drain("vector")
                for dc in range(NDC):
                    p.op("vector", "tensor_scalar",
                         wpg[:, c0 + dc * 128: c0 + dc * 128 + 128],
                         wfpg[:, dc * 128:(dc + 1) * 128],
                         self.vecs[:, VEC_COLS[s_name] + dc:VEC_COLS[s_name] + dc + 1],
                         scale, op0=ALU.mult, op1=ALU.mult,
                         sem="DVE" if dc == NDC - 1 else None)
                fold_dep = p.cnt["DVE"]
                if wi < 2:
                    p.wait("sync", "DVE", fold_dep)
            wq_dep = ("DVE", fold_dep)
        else:
            half = g % 2
            if g >= 2:
                p.wait("gpsimd", "PE", self.group_core_pe)
            d = p.op("gpsimd", "dma_start",
                     wpg[:, half * 1024:(half + 1) * 1024]
                     .rearrange("p (dc c) -> p dc c", dc=NDC),
                     self.w_h["bWq"][:, g * 128:(g + 1) * 128]
                     .rearrange("(dc p) c -> p dc c", p=128),
                     sem="bq" + "AB"[half], k=16)
            for wi, wh in enumerate((self.w_h["bWupK"], self.w_h["bWupV"])):
                p.op("gpsimd", "dma_start",
                     wupg[:, wi * 256:(wi + 1) * 256]
                     .rearrange("p (lc c) -> p lc c", lc=2),
                     wh[:, g * 128:(g + 1) * 128]
                     .rearrange("(lc p) c -> p lc c", p=128),
                     sem="bq" + "AB"[half], k=16)
            wq_dep = ("bq" + "AB"[half], p.cnt["bq" + "AB"[half]])

        csq = self.csPool[0:1, 0:1024]
        csk = self.csPool[32:33, 0:1024]
        csv = self.csPool[64:65, 0:1024]
        evac_dep = {}
        for tq in range(NT8):
            tqs = slice(tq * 512, (tq + 1) * 512)
            if a:
                mu_row = ws.mu_bro[0:1, tqs]
                mu_row32 = ws.mu_bro[32:33, tqs]
                mu_row64 = ws.mu_bro[64:65, tqs]
                Rmul = ws.R_bro[:, tqs]
            else:
                sdep = self.stat_bcast_tq(ws, tq)
                mu_row = ws.murow[0:1, :]
                mu_row32 = ws.murow[32:33, :]
                Rmul = ws.Rb_sb[:, :]
                p.wait("tensor", "DVE", sdep)
            p.waitd("vector", getattr(self, "xbf_free", None))
            p.drain("vector")
            for dc in range(NDC):
                p.op("vector", "tensor_copy", xbf[:, dc * 512:(dc + 1) * 512],
                     self.st(dc, tq * 512, 512),
                     sem="DVE" if dc == NDC - 1 else None)
            xcast = p.cnt["DVE"]
            if tq > 0:
                p.wait("tensor", "DVE", evac_dep.get(tq - 1, 0))
            p.wait("tensor", "DVE", xcast)
            if tq == 0:
                p.waitd("tensor", wq_dep)
            for dc in range(NDC):
                xs = xbf[:, dc * 512:(dc + 1) * 512]
                if a:
                    p.op("tensor", "matmul", self.bank(0),
                         wpg[:, dc * 128:(dc + 1) * 128], xs,
                         start=(dc == 0), stop=False)
                    p.op("tensor", "matmul", self.bank(1),
                         wpg[:, 1024 + dc * 128: 1024 + dc * 128 + 128], xs,
                         start=(dc == 0), stop=False)
                    for tt in range(4):
                        p.op("tensor", "matmul", self.bank(2 + tt, 128),
                             xs[:, tt * 128:(tt + 1) * 128],
                             wpg[:, 2048 + dc * 128: 2048 + dc * 128 + 128],
                             start=(dc == 0), stop=False)
                else:
                    p.op("tensor", "matmul", self.bank(0),
                         wpg[:, (g % 2) * 1024 + dc * 128:(g % 2) * 1024 + dc * 128 + 128],
                         xs, start=(dc == 0), stop=False)
            if a:
                p.op("tensor", "matmul", self.bank(0),
                     csq[:, g * 128:(g + 1) * 128], mu_row,
                     start=False, stop=True)
                p.op("tensor", "matmul", self.bank(1),
                     csk[:, g * 128:(g + 1) * 128], mu_row32,
                     start=False, stop=True)
                for tt in range(4):
                    p.op("tensor", "matmul", self.bank(2 + tt, 128),
                         mu_row64[:, tt * 128:(tt + 1) * 128],
                         csv[:, g * 128:(g + 1) * 128],
                         start=False, stop=True,
                         sem="PE" if tt == 3 else None)
            else:
                p.op("tensor", "matmul", self.bank(0),
                     csq[:, g * 128:(g + 1) * 128], mu_row,
                     start=False, stop=True)
                for lc in range(2):
                    p.op("tensor", "matmul", self.bank(1),
                         wupg[:, lc * 128: lc * 128 + 128],
                         self.lat[:, lc * T + tq * 512: lc * T + tq * 512 + 512],
                         start=(lc == 0), stop=(lc == 1))
                for tt in range(4):
                    for lc in range(2):
                        p.op("tensor", "matmul", self.bank(2 + tt, 128),
                             self.lat[:, lc * T + tq * 512 + tt * 128:
                                      lc * T + tq * 512 + tt * 128 + 128],
                             wupg[:, 256 + lc * 128: 256 + lc * 128 + 128],
                             start=(lc == 0), stop=(lc == 1),
                             sem="PE" if tt == 3 else None)
            mm_all = p.cnt["PE"]
            self.xbf_free = ("PE", mm_all)
            p.wait("vector", "PE", mm_all)
            if a:
                p.wait("vector", "ACT", getattr(self, "qsc_act_free", 0))
                p.drain("vector")
                v = p.op("vector", "tensor_tensor", qsc, self.bank(0),
                         Rmul, op=ALU.mult, sem="DVE")
                p.wait("scalar", "DVE", v)
                p.drain("scalar")
                aq = p.op("scalar", "activation", Qg[:, tqs], qsc, AF.Identity,
                          bias=self.bcol("bq", g), sem="ACT")
                p.wait("vector", "ACT", aq)
                p.drain("vector")
                v = p.op("vector", "tensor_tensor", qsc, self.bank(1),
                         Rmul, op=ALU.mult, sem="DVE")
                p.wait("scalar", "DVE", v)
                p.drain("scalar")
                ak = p.op("scalar", "activation", Kg[:, tqs], qsc, AF.Identity,
                          bias=self.bcol("bk", g), sem="ACT")
                for tt in range(4):
                    kc = tq * 4 + tt
                    p.drain("vector")
                    p.op("vector", "tensor_scalar",
                         Vb[:, kc * 130: kc * 130 + 64],
                         self.bank(2 + tt, 64),
                         self.R_tok[:, kc:kc + 1], None, op0=ALU.mult)
                    p.drain("vector")
                    p.op("vector", "tensor_scalar",
                         Vb[:, kc * 130 + 65: kc * 130 + 129],
                         self.bank(2 + tt, 64, off=64),
                         self.R_tok[:, kc:kc + 1], None, op0=ALU.mult,
                         sem="DVE" if tt == 3 else None)
                self.norms_pass(g, tq, Qg[:, tqs], "q", aq, bf=True)
                self.norms_pass(g, tq, Kg[:, tqs], "k", ak, bf=True)
                self.qsc_act_free = ak
            else:
                p.wait("vector", "ACT", getattr(self, "qsc_act_free", 0))
                p.drain("vector")
                v = p.op("vector", "tensor_tensor", qsc, self.bank(0),
                         Rmul, op=ALU.mult, sem="DVE")
                p.drain("vector")
                qpc = p.op("vector", "tensor_copy", Qg[:, tqs], qsc, sem="DVE")
                self.norms_pass(g, tq, qsc, "q", 0, bf=False, ddep=qpc)
                self.norms_pass(g, tq, self.bank(1), "k", 0, bf=False, ddep=qpc)
                self.qsc_act_free = p.cnt["ACT"]
                p.drain("vector")
                p.op("vector", "tensor_copy", Kg[:, tqs], self.bank(1), sem="DVE")
                for tt in range(4):
                    kc = tq * 4 + tt
                    p.drain("vector")
                    p.op("vector", "tensor_copy", Vb[:, kc * 130: kc * 130 + 64],
                         self.bank(2 + tt, 64))
                    p.drain("vector")
                    p.op("vector", "tensor_copy",
                         Vb[:, kc * 130 + 65: kc * 130 + 129],
                         self.bank(2 + tt, 64, off=64),
                         sem="DVE" if tt == 3 else None)
            evac_dep[tq] = p.cnt["DVE"]
        self.group_evac_done = p.cnt["DVE"]
        self.group_evac_act = p.cnt["ACT"]

    def norms_pass(self, g, tq, src, which, adep, bf, ddep=0):
        p = self.p
        sqn = self._sqn if hasattr(self, "_sqn") else None
        if which == "q":
            base = RW_QC
        else:
            base = RW_KC
        if adep:
            p.wait("scalar", "ACT", adep)
        if ddep:
            p.wait("scalar", "DVE", ddep)
        p.wait("scalar", "PE", p.cnt["PE"])
        p.drain("scalar")
        s = p.op("scalar", "activation", self.nsqn, src, AF.Square, sem="ACT")
        p.wait("tensor", "ACT", s)
        nb = 6 if self._variant_a else 7
        p.wait("tensor", "DVE", p.cnt["DVE"])
        m = p.op("tensor", "matmul", self.bank(nb, 512, p=2),
                 self.loc2[:, :], self.nsqn[:, :],
                 start=True, stop=True, sem="PE")
        p.wait("vector", "PE", m)
        c0 = base + 2 * (g % 4)
        p.drain("vector")
        p.op("vector", "tensor_reduce", self.cw(c0, 1, p=2),
             self.bank(nb, 512, p=2), axis=AX.X, op=ALU.max)
        p.drain("vector")
        if tq == 0:
            p.op("vector", "tensor_copy", self.cw(c0 + 1, 1, p=2),
                 self.cw(c0, 1, p=2), sem="DVE")
        else:
            p.op("vector", "tensor_tensor", self.cw(c0 + 1, 1, p=2),
                 self.cw(c0 + 1, 1, p=2), self.cw(c0, 1, p=2), op=ALU.max,
                 sem="DVE")
        p.drain("vector")

    def finalize_M(self, g):
        p = self.p
        cell = self.cw(RW_MC + 2 * (g % 4), 1, p=2)
        p.drain("vector")
        v = p.op("vector", "tensor_tensor", cell,
                 self.cw(RW_QC + 2 * (g % 4) + 1, 1, p=2),
                 self.cw(RW_KC + 2 * (g % 4) + 1, 1, p=2), op=ALU.mult, sem="DVE")
        p.wait("scalar", "DVE", v)
        p.drain("scalar")
        p.op("scalar", "activation", cell, cell, AF.Ln, bias=self.constA[0:2, 1:2])
        p.drain("scalar")
        s = p.op("scalar", "activation", cell, cell, AF.Exp, scale=0.5, sem="ACT")
        p.wait("vector", "ACT", s)
        p.drain("vector")
        sc = -1.0 if self._variant_a else -INV32
        v = p.op("vector", "tensor_scalar", cell, cell, sc, None, op0=ALU.mult,
                 sem="DVE")
        p.wait("tensor", "DVE", v)
        t = p.op("tensor", "transpose", self.bank(7, 2, off=256, p=1),
                 cell, self.ident_f[0:2, 0:2], sem="PE")
        p.wait("vector", "PE", t)
        p.drain("vector")
        cv = p.op("vector", "tensor_copy", self.cw(RW_MT, 2),
                  self.bank(7, 2, off=256, p=1), sem="DVE")
        p.wait("tensor", "DVE", cv)
        for loc in range(2):
            p.op("tensor", "matmul", self.bank(7, 1, off=loc),
                 self.ones_f[0:1, :], self.cw(RW_MT + loc, 1),
                 start=True, stop=True, sem="PE" if loc == 1 else None)
        m = p.cnt["PE"]
        p.wait("vector", "PE", m)
        p.drain("vector")
        md = p.op("vector", "tensor_copy", self.Mneg[:, 2 * g:2 * g + 2],
                  self.bank(7, 2), sem="DVE")
        p.drain("vector")
        return md

    def zero_qpad(self, loc, bufs):
        p = self.p
        orl = slice(64 * (1 - loc), 64 * (1 - loc) + 64)
        p.wait("vector", "PE", self.qtw_pe)
        p.drain("vector")
        for bu in bufs:
            p.op("vector", "memset", bu[orl, 0:1024], 0.0, sem="DVE")

    def build_qw(self, loc, h, w, Qg, qtw, qhw, a):
        p = self.p
        rl = slice(64 * loc, 64 * loc + 64)
        wq = slice(w * 512, w * 512 + 512)
        b = (w % 2) * 512
        p.wait("vector", "PE", self.qtw_pe)
        if a:
            p.drain("vector")
            self.qw_dep = p.op("vector", "tensor_copy", qtw[rl, b:b + 512],
                               Qg[rl, wq], sem="DVE")
            return
        im = math.exp(-LAM * 512 * w)
        ip = math.exp(LAM * 512 * w)
        p.wait("vector", "ACT", self.decay_dep[0])
        p.drain("vector")
        p.op("vector", "scalar_tensor_tensor", qtw[rl, b:b + 512],
             self.E0mb[rl, :], im, Qg[rl, wq], op0=ALU.mult, op1=ALU.mult)
        p.drain("vector")
        self.qw_dep = p.op("vector", "scalar_tensor_tensor", qhw[rl, b:b + 512],
                           self.E0pb[rl, :], ip, Qg[rl, wq],
                           op0=ALU.mult, op1=ALU.mult, sem="DVE")
        if h == 0 and w == 0:
            self.tap("qtw0", qtw[0:64, 0:512])
            self.tap("qhw0", qhw[0:64, 0:512])
            self.tap("Qg0", Qg[0:64, 0:512])

    def flush_normalize(self, atts, attst):
        p = self.p
        st = self.defer_norm
        if st is None:
            return
        self.defer_norm = None
        h, w, ab, pv = st
        p.wait("vector", "PE", max(pv, getattr(self, "rb_mm_last", 0)))
        p.drain("vector")
        v = p.op("vector", "reciprocal", self.rw(RW_RCP, 512),
                 self.P8[64:65, ab * 512: ab * 512 + 512], sem="DVE")
        p.wait("tensor", "DVE", v)
        m = p.op("tensor", "matmul", self.P8[0:64, 6 * 512:6 * 512 + 512],
                 self.ones_f[0:1, 0:64], self.rw(RW_RCP, 512),
                 start=True, stop=True, sem="PE")
        self.rb_mm_last = m
        p.wait("scalar", "PE", pv)
        p.wait("scalar", "DVE", getattr(self, "z_last", 0))
        p.drain("scalar")
        s = p.op("scalar", "activation", atts[0:64, :],
                 self.P8[0:64, ab * 512: ab * 512 + 512], AF.Copy, sem="ACT")
        p.wait("vector", "PE", m)
        p.wait("vector", "ACT", s)
        buf = w % 2
        stg = attst[0:64, buf * 512: buf * 512 + 512]
        p.waitd("vector", self.attst_dma.get(buf))
        p.drain("vector")
        z = p.op("vector", "tensor_tensor", stg, atts[0:64, :],
                 self.P8[0:64, 6 * 512:6 * 512 + 512], op=ALU.mult, sem="DVE")
        self.z_last = z
        self.attb_free[ab] = (s, z)
        p.wait("sync", "DVE", z)
        dsv = p.op("sync", "dma_start",
                   self.attscr[h * 64:(h + 1) * 64, w * 512: w * 512 + 512],
                   attst[0:64, buf * 512: buf * 512 + 512],
                   sem="as" + "AB"[buf], k=16)
        self.attst_dma[buf] = ("as" + "AB"[buf], dsv)

    def attn_core(self, ws, g, loc, h, w, variant, Qg, Kg, Vb, PT, atts,
                  attst, qtw, qhw, Ew, mdep):
        p, T = self.p, self.T
        a = variant == "a"
        NKC = self.NKC
        rl = slice(64 * loc, 64 * loc + 64)
        wq = slice(w * 512, w * 512 + 512)
        if w == 0:
            self.zero_qpad(loc, (qtw,) if a else (qtw, qhw))
            self.build_qw(loc, h, 0, Qg, qtw, qhw, a)
        qdep = self.qw_dep

        p.wait("tensor", "ACT", self.group_evac_act)
        p.wait("tensor", "DVE", self.group_evac_done)
        NKH = NKC // 2
        b2 = (w % 2) * 512
        ab = 4 + (w % 2)
        fr = self.attb_free.get(ab)
        if fr:
            p.wait("tensor", "ACT", fr[0])
            p.wait("tensor", "DVE", fr[1])
        p.wait("tensor", "DVE", self.vb_ones)
        for hp in range(2):
            exp_cnt = {}
            for jj in range(NKH):
                kc = hp * NKH + jj
                bb = jj % 4
                if len(self.quad_hist) >= 4:
                    p.wait("tensor", "ACT", self.quad_hist[-4])
                dst = self.P8[:, bb * 512: bb * 512 + 512]
                fix = 0
                if qdep and jj == 0 and hp == 0:
                    p.wait("tensor", "DVE", qdep)
                if a:
                    m = p.op("tensor", "matmul", dst,
                             Kg[:, kc * 128:(kc + 1) * 128],
                             qtw[:, b2:b2 + 512],
                             start=True, stop=True, sem="PE")
                    if kc == NKC - 1 and w == 0:
                        p.wait("vector", "PE", m)
                        p.drain("vector")
                        sl = self.P8[64:128, bb * 512: bb * 512 + 1]
                        fix = p.op("vector", "tensor_tensor", sl, sl,
                                   self.maskc[64:128, :], op=ALU.add, sem="DVE")
                else:
                    o = (kc - 4 * w) * 128
                    if o <= -128 or o >= 512:
                        rhs = (qtw[:, b2:b2 + 512] if o <= -128
                               else qhw[:, b2:b2 + 512])
                        m = p.op("tensor", "matmul", dst,
                                 Kg[:, kc * 128:(kc + 1) * 128], rhs,
                                 start=True, stop=True, sem="PE")
                    else:
                        m = p.op("tensor", "matmul", dst,
                                 Kg[:, kc * 128:(kc + 1) * 128],
                                 qtw[:, b2:b2 + 512],
                                 start=True, stop=True, sem="PE")
                        p.wait("vector", "PE", m)
                        p.wait("vector", "DVE", self.decay_dep[1])
                        p.drain("vector")
                        fix = p.op("vector", "tensor_tensor", dst, dst,
                                   self.CM[:, 384 - o:384 - o + 512],
                                   op=ALU.mult, sem="DVE")
                        if h == 0 and w == 0 and kc == 0:
                            self.tap("Kg0", Kg[:, 0:128])
                mm = p.cnt["PE"]
                p.wait("scalar", "PE", mm)
                if fix:
                    p.wait("scalar", "DVE", fix)
                if jj == 0 and hp == 0:
                    p.wait("scalar", "DVE", mdep)
                if jj == 0:
                    p.wait("scalar", "PE", getattr(self, "pt_free", 0))
                if a:
                    p.drain("scalar")
                    e = p.op("scalar", "activation", PT[:, jj * 512:(jj + 1) * 512],
                             dst, AF.Exp, bias=self.Mneg[:, h:h + 1], sem="ACT")
                else:
                    p.wait("scalar", "DVE", self.vpm_dep)
                    vcol = (self.vpm[:, kc:kc + 1] if kc <= 4 * w + 3
                            else self.vpm[:, 32 + kc:32 + kc + 1])
                    p.drain("scalar")
                    e = p.op("scalar", "activation", PT[:, jj * 512:(jj + 1) * 512],
                             dst, AF.Exp, bias=self.Mneg[:, h:h + 1], scale=vcol,
                             sem="ACT")
                    if h == 0 and w == 0 and hp == 0 and jj == 0:
                        self.tap("exp00", PT[:, 0:512])
                        self.tap("Mneg0", self.Mneg[:, 0:1])
                exp_cnt[jj] = e
                self.quad_hist.append(e)
            for jj in range(NKH):
                kc = hp * NKH + jj
                p.wait("tensor", "ACT", exp_cnt[jj])
                p.op("tensor", "matmul", self.P8[0:65, ab * 512: ab * 512 + 512],
                     Vb[:, kc * 130 + 65 * loc: kc * 130 + 65 * loc + 65],
                     PT[:, jj * 512:(jj + 1) * 512],
                     start=(kc == 0), stop=(kc == NKC - 1),
                     sem="PE" if jj == NKH - 1 else None)
            pv = p.cnt["PE"]
            self.pt_free = pv
        prev = self.defer_norm
        self.defer_norm = (h, w, ab, pv)
        if prev is not None:
            self.defer_norm = prev
            self.flush_normalize(atts, attst)
            self.defer_norm = (h, w, ab, pv)
        if w + 1 < self.NW:
            self.build_qw(loc, h, w + 1, Qg, qtw, qhw, a)
        if w == self.NW - 1:
            self.flush_normalize(atts, attst)
        self.group_core_pe = p.cnt["PE"]
        self.group_core_dve = p.cnt["DVE"]
        self.qtw_pe = p.cnt["PE"]

    def y0_add(self, ws):
        p, NT8 = self.p, self.NT8
        with ExitStack() as ses:
            tA = self.sb("y0_tA", [128, 512], F32, ses)
            for t8 in range(NT8):
                bc = self.bcast_cols(self.r1_tok[:, t8 * 4:(t8 + 1) * 4], 4, 6)
                p.wait("vector", "PE", bc)
                for dc in range(NDC):
                    xs = self.st(dc, t8 * 512, 512)
                    p.drain("vector")
                    p.op("vector", "tensor_tensor", tA, xs,
                         ws.mu_bro[:, t8 * 512:(t8 + 1) * 512], op=ALU.add)
                    p.drain("vector")
                    p.op("vector", "tensor_tensor", tA, tA, self.bank(6),
                         op=ALU.mult)
                    p.drain("vector")
                    p.op("vector", "tensor_tensor", xs, xs, tA, op=ALU.add,
                         sem="DVE" if dc == NDC - 1 else None)
                p.wait("tensor", "DVE", p.cnt["DVE"])
            p.drain("vector")

    def wo_phase(self, ws, wo_h, with_bias):
        p, T = self.p, self.T
        half_T = T // 2
        n_t8h = half_T // 512
        wn = "woa" if with_bias else "wob"
        with ExitStack() as ses:
            attb = self.sb(wn + "_att", [128, NDC * half_T], BF16, ses)
            wob = self.sb(wn + "_w", [128, 2048], BF16, ses)
            bvb = self.sb(wn + "_bvb", [128, 8], BF16, ses)
            if with_bias:
                p.drain("vector")
                bv = p.op("vector", "tensor_copy", bvb[:, :],
                          self.biasA[:, BA["bv"]:BA["bv"] + 8], sem="DVE")
            wodb = DB(2)
            evdb = DB(2)
            jd = 0
            je = 0
            for th in range(2):
                if th == 0:
                    p.waitd("sync", self.attst_dma.get(0))
                    p.waitd("sync", self.attst_dma.get(1))
                adma = 0
                for fc in range(NDC):
                    if th > 0 and fc == 0:
                        p.wait("sync", "PE", self.wo_half_pe)
                    adma = p.op("sync", "dma_start",
                                attb[:, fc * half_T:(fc + 1) * half_T],
                                self.attscr[fc * 128:(fc + 1) * 128,
                                            th * half_T:(th + 1) * half_T],
                                sem="wat", k=16)
                for dco in range(NDC):
                    b = wodb.slot(jd)
                    p.waitd("gpsimd", wodb.reuse_dep(jd))
                    d = p.op("gpsimd", "dma_start",
                             wob[:, b * 1024:(b + 1) * 1024]
                             .rearrange("p (fc c) -> p fc c", fc=NDC),
                             wo_h[:, dco * 128:(dco + 1) * 128]
                             .rearrange("(fc p) c -> p fc c", p=128),
                             sem="wo" + "AB"[b], k=16)
                    p.wait("tensor", "wo" + "AB"[b], d)
                    p.wait("tensor", "wat", adma)
                    if with_bias and th == 0:
                        p.wait("tensor", "DVE", bv)
                        for fc in range(NDC):
                            p.op("tensor", "matmul", self.bank(2, 1),
                                 wob[:, b * 1024 + fc * 128: b * 1024 + fc * 128 + 128],
                                 bvb[:, fc:fc + 1],
                                 start=(fc == 0), stop=(fc == NDC - 1),
                                 sem="PE" if fc == NDC - 1 else None)
                        mb = p.cnt["PE"]
                        p.wait("vector", "PE", mb)
                        p.drain("vector")
                        p.op("vector", "tensor_scalar",
                             self.biasA[:, BA["bo"] + dco:BA["bo"] + dco + 1],
                             self.bank(2, 1), self.vcol("bo", dco), None,
                             op0=ALU.add, sem="DVE")
                        p.wait("tensor", "DVE", p.cnt["DVE"])
                    for t8 in range(n_t8h):
                        bw = t8 % 2
                        p.waitd("tensor", self.wo_evfree.get(bw) if hasattr(self, "wo_evfree") else None)
                        for fc in range(NDC):
                            p.op("tensor", "matmul", self.bank(bw),
                                 wob[:, b * 1024 + fc * 128: b * 1024 + fc * 128 + 128],
                                 attb[:, fc * half_T + t8 * 512: fc * half_T + t8 * 512 + 512],
                                 start=(fc == 0), stop=(fc == NDC - 1),
                                 sem="PE")
                        mm = p.cnt["PE"]
                        p.wait("vector", "PE", mm)
                        xs = self.st(dco, th * half_T + t8 * 512, 512)
                        p.drain("vector")
                        if with_bias:
                            e = p.op("vector", "scalar_tensor_tensor", xs,
                                     self.bank(bw),
                                     self.biasA[:, BA["bo"] + dco:BA["bo"] + dco + 1],
                                     xs, op0=ALU.add, op1=ALU.add, sem="DVE")
                        else:
                            e = p.op("vector", "tensor_tensor", xs, self.bank(bw),
                                     xs, op=ALU.add, sem="DVE")
                        if not hasattr(self, "wo_evfree"):
                            self.wo_evfree = {}
                        self.wo_evfree[bw] = ("DVE", e)
                        je += 1
                    wodb.set_done(jd, ("PE", p.cnt["PE"]))
                    jd += 1
                self.wo_half_pe = p.cnt["PE"]
            p.drain("vector")

    def ffn_like(self, ws, ses, kind, call=0):
        p, T = self.p, self.T
        moe = kind == "moe"
        NCH = T // 512
        NI = (IMOE if moe else IMLP) // 128
        NIG = NI // 2
        xhat = self.sb(f"f_xh{kind}{call}", [128, NDC * 512], BF16, ses)
        H = self.sb(f"f_H{kind}{call}", [128, NI * 512], BF16, ses)
        w1b = self.sb(f"f_w1{kind}{call}", [128, 4096], BF16, ses)
        w2b = self.sb(f"f_w2{kind}{call}", [128, 4096], BF16, ses)
        wdb_t = self.sb(f"f_wd{kind}{call}", [128, 4096], BF16, ses) if moe else None
        wfp = self.sb(f"f_wf{kind}{call}", [128, 2048], F32, ses) if not moe else None
        comb = self.rw(0, 64, p=128) if moe else None
        cbro = self.sb(f"f_cw{kind}{call}", [128, 512], F32, ses) if moe else None
        scrF = self.sb(f"f_sc{kind}{call}", [128, 1024 if moe else 512], F32, ses)
        gsc = scrF[:, 0:512]
        tsc = scrF[:, 512:1024] if moe else scrF[:, 0:512]
        if moe:
            xf32 = scrF[:, 0:1024]
            tmpg = self.rw(RW_TMPG, 128, p=128)
            self.gate_ws = (ws, xf32, tmpg)
        if not moe:
            self.mlp_bias_prep(ws, wfp, w1b)
            self.p.barrier()
        gdb = DB(2)
        ddb = DB(2)
        sdb = DB(2)
        jg = 0
        jd = 0
        js = 0
        prev_wd_pe = 0
        prev_chunk_pe = 0
        for ch in range(NCH):
            ts0 = ch * 512
            p.wait("tensor", "DVE", p.cnt["DVE"])
            p.wait("tensor", "DVE", self.stats_done)
            self.bcast_cols(self.mu_tok[:, ch * 4:(ch + 1) * 4], 4, 6, tb=4)
            bc = self.bcast_cols(self.R_tok[:, ch * 4:(ch + 1) * 4], 4, 7, tb=4)
            mub = self.bank(6)
            Rb = self.bank(7)
            p.wait("vector", "PE", max(prev_chunk_pe, bc))
            for dc in range(NDC):
                p.drain("vector")
                p.op("vector", "tensor_tensor", tsc, self.st(dc, ts0, 512),
                     mub, op=ALU.add)
                p.drain("vector")
                p.op("vector", "tensor_tensor",
                     xhat[:, dc * 512:(dc + 1) * 512], tsc,
                     Rb, op=ALU.mult,
                     sem="DVE" if dc == NDC - 1 else None)
            xh_done = p.cnt["DVE"]
            gate_done = self.moe_gating(ch, xhat, comb, xh_done) if moe else 0
            for e in range(NE if moe else 1):
                silu_hist = []
                for ig in range(NIG):
                    b = gdb.slot(jg)
                    gs = f"g{kind}{call}" + "AB"[b]
                    if moe:
                        p.waitd("gpsimd", gdb.reuse_dep(jg))
                        if (call, ch, e, ig) == (0, 0, 0, 0):
                            for dep in self.transcode_done:
                                p.waitd("gpsimd", dep)
                        p.op("gpsimd", "dma_start",
                             w1b[:, b * 2048:(b + 1) * 2048]
                             .rearrange("p (dc i) -> p dc i", dc=NDC),
                             self.wgB[e, :, ig * 256:(ig + 1) * 256]
                             .rearrange("(dc p) i -> p dc i", p=128),
                             sem=gs, k=16)
                        d2 = p.op("gpsimd", "dma_start",
                                  w2b[:, b * 2048:(b + 1) * 2048]
                                  .rearrange("p (dc i) -> p dc i", dc=NDC),
                                  self.wuB[e, :, ig * 256:(ig + 1) * 256]
                                  .rearrange("(dc p) i -> p dc i", p=128),
                                  sem=gs, k=16)
                        wdep = (gs, d2)
                    else:
                        p.waitd("sync", getattr(self, "mlp_fold_done", None))
                        d1 = p.op("sync", "dma_start",
                                  wfp[:, :]
                                  .rearrange("p (dc i) -> p dc i", dc=NDC),
                                  self.w_h["aW1"][:, ig * 256:(ig + 1) * 256]
                                  .rearrange("(dc p) i -> p dc i", p=128),
                                  sem=gs, k=16)
                        p.wait("vector", gs, d1)
                        p.waitd("vector", gdb.reuse_dep(jg))
                        p.drain("vector")
                        for dc in range(NDC):
                            p.op("vector", "tensor_scalar",
                                 w1b[:, b * 2048 + dc * 256: b * 2048 + dc * 256 + 256],
                                 wfp[:, dc * 256: dc * 256 + 256],
                                 self.vecs[:, VEC_COLS["lnm_s"] + dc:
                                           VEC_COLS["lnm_s"] + dc + 1],
                                 None, op0=ALU.mult,
                                 sem="DVE" if dc == NDC - 1 else None)
                        wdep = ("DVE", p.cnt["DVE"])
                        self.mlp_fold_done = wdep
                    for ic in range(2):
                        i = ig * 2 + ic
                        bG = (i % 2) * 2
                        bU = bG + 1
                        p.waitd("tensor", wdep)
                        if ig == 0 and ic == 0:
                            p.wait("tensor", "DVE", xh_done)
                            p.wait("tensor", "PE", prev_wd_pe)
                        if len(silu_hist) >= 2:
                            p.wait("tensor", "ACT", silu_hist[-2][0])
                            p.wait("tensor", "DVE", silu_hist[-2][1])
                        for dc in range(NDC):
                            co = b * 2048 + dc * 256 + ic * 128
                            p.op("tensor", "matmul", self.bank(bG),
                                 w1b[:, co:co + 128],
                                 xhat[:, dc * 512:(dc + 1) * 512],
                                 start=(dc == 0), stop=(dc == NDC - 1),
                                 sem="PE" if (not moe and dc == NDC - 1) else None)
                            if moe:
                                p.op("tensor", "matmul", self.bank(bU),
                                     w2b[:, co:co + 128],
                                     xhat[:, dc * 512:(dc + 1) * 512],
                                     start=(dc == 0), stop=(dc == NDC - 1),
                                     sem="PE" if dc == NDC - 1 else None)
                        mm = p.cnt["PE"]
                        p.wait("scalar", "PE", mm)
                        if moe:
                            if silu_hist:
                                p.wait("scalar", "DVE", silu_hist[-1][1])
                            p.drain("scalar")
                            sl = p.op("scalar", "activation", gsc, self.bank(bG),
                                      AF.Silu, sem="ACT")
                            p.wait("vector", "ACT", sl)
                            p.drain("vector")
                            hm = p.op("vector", "tensor_tensor",
                                      H[:, i * 512:(i + 1) * 512], gsc,
                                      self.bank(bU), op=ALU.mult, sem="DVE")
                            silu_hist.append((sl, hm))
                        else:
                            p.drain("scalar")
                            sl = p.op("scalar", "activation",
                                      H[:, i * 512:(i + 1) * 512], self.bank(bG),
                                      AF.Gelu_apprx_tanh,
                                      bias=self.bcol("b1", i), sem="ACT")
                            silu_hist.append((sl, 0))
                    gdb.set_done(jg, ("PE", p.cnt["PE"]))
                    jg += 1
                h_act = p.cnt["ACT"]
                h_dve = p.cnt["DVE"]
                cb_done = self.moe_cbro(ch, e, comb, cbro, gate_done) if moe else 0
                for dco in range(NDC):
                    bW = 4 + (dco % 2)
                    p.waitd("tensor", sdb.reuse_dep(js))
                    if moe and dco < 2:
                        p.wait("tensor", "DVE", cb_done)
                    for ihalf in range(2):
                        b = ddb.slot(jd)
                        nic = NI // 2
                        ds2 = f"d{kind}{call}" + "AB"[b]
                        if moe:
                            p.waitd("gpsimd", ddb.reuse_dep(jd))
                            d = p.op("gpsimd", "dma_start",
                                     wdb_t[:, b * 2048:(b + 1) * 2048]
                                     .rearrange("p (r c) -> p r c", r=16),
                                     self.wdB[e, ihalf * (NI * 64):(ihalf * (NI * 64)) + NI * 64,
                                              dco * 128:(dco + 1) * 128]
                                     .rearrange("(r p) c -> p r c", p=128),
                                     sem=ds2, k=16)
                            wdep2 = (ds2, d)
                            wsrc = wdb_t
                        else:
                            p.waitd("gpsimd", ddb.reuse_dep(jd))
                            d = p.op("gpsimd", "dma_start",
                                     w2b[:, b * 2048: b * 2048 + nic * 128]
                                     .rearrange("p (r c) -> p r c", r=nic),
                                     self.w_h["aW2"][ihalf * (NI * 64):(ihalf * (NI * 64)) + NI * 64,
                                                     dco * 128:(dco + 1) * 128]
                                     .rearrange("(r p) c -> p r c", p=128),
                                     sem=ds2, k=16)
                            wdep2 = (ds2, d)
                            wsrc = w2b
                        p.waitd("tensor", wdep2)
                        if dco == 0 and ihalf == 0:
                            p.wait("tensor", "ACT", h_act)
                            p.wait("tensor", "DVE", h_dve)
                        for r in range(nic):
                            i = ihalf * nic + r
                            p.op("tensor", "matmul", self.bank(bW),
                                 wsrc[:, b * 2048 + r * 128: b * 2048 + r * 128 + 128],
                                 H[:, i * 512:(i + 1) * 512],
                                 start=(ihalf == 0 and r == 0),
                                 stop=(ihalf == 1 and r == nic - 1),
                                 sem="PE" if r == nic - 1 else None)
                        ddb.set_done(jd, ("PE", p.cnt["PE"]))
                        jd += 1
                    mm = p.cnt["PE"]
                    p.wait("vector", "PE", mm)
                    if moe:
                        p.wait("vector", "DVE", cb_done)
                    xs = self.st(dco, ch * 512, 512)
                    p.drain("vector")
                    if moe:
                        p.op("vector", "tensor_tensor", tsc, self.bank(bW),
                             cbro[:, :], op=ALU.mult)
                        p.drain("vector")
                        ev = p.op("vector", "tensor_tensor", xs, xs, tsc,
                                  op=ALU.add, sem="DVE")
                    else:
                        ev = p.op("vector", "scalar_tensor_tensor", xs,
                                  self.bank(bW), self.vcol("b2", dco), xs,
                                  op0=ALU.add, op1=ALU.add, sem="DVE")
                    sdb.set_done(js, ("DVE", ev))
                    js += 1
                prev_wd_pe = p.cnt["PE"]
            prev_chunk_pe = p.cnt["PE"]
        p.drain("vector")
        p.drain("scalar")

    def mlp_bias_prep(self, ws, wfp, w2b):
        p = self.p
        with ExitStack() as ses:
            bcolb = self.sb("m_bcol", [128, 8], BF16, ses)
            p.drain("vector")
            cb = p.op("vector", "tensor_copy", bcolb[:, :],
                      self.vecs[:, VEC_COLS["lnm_b"]:VEC_COLS["lnm_b"] + 8],
                      sem="DVE")
            wdbl = DB(2)
            jj = 0
            for rnd in range(2):
                for dc in range(NDC):
                    b = wdbl.slot(jj)
                    p.waitd("sync", wdbl.reuse_dep(jj))
                    d = p.op("sync", "dma_start", wfp[:, b * 1024: b * 1024 + 1024],
                             self.w_h["aW1"][dc * 128:(dc + 1) * 128,
                                             rnd * 1024:(rnd + 1) * 1024],
                             sem="mb" + "AB"[b], k=16)
                    p.wait("vector", "mb" + "AB"[b], d)
                    p.drain("vector")
                    p.waitd("vector", getattr(self, "mlpb_free", None))
                    c = p.op("vector", "tensor_copy",
                             w2b[:, b * 1024: b * 1024 + 1024],
                             wfp[:, b * 1024: b * 1024 + 1024], sem="DVE")
                    wdbl.set_done(jj, ("DVE", c))
                    p.wait("tensor", "DVE", max(c, cb))
                    for oc in range(NDC):
                        p.op("tensor", "matmul", self.bank(oc, 1),
                             w2b[:, b * 1024 + oc * 128: b * 1024 + oc * 128 + 128],
                             bcolb[:, dc:dc + 1],
                             start=(dc == 0), stop=(dc == NDC - 1),
                             sem="PE" if oc == NDC - 1 else None)
                    if jj >= 1:
                        self.mlpb_free = ("PE", p.cnt["PE"])
                    jj += 1
                done = p.cnt["PE"]
                p.wait("vector", "PE", done)
                for oc in range(NDC):
                    i16 = rnd * 8 + oc
                    p.drain("vector")
                    p.op("vector", "tensor_scalar",
                         self.biasA[:, BA["b1"] + i16:BA["b1"] + i16 + 1],
                         self.bank(oc, 1), self.vcol("b1", i16), None,
                         op0=ALU.add, sem="DVE")
            p.drain("vector")

    def moe_gating(self, ch, xhat, comb, xh_done):
        p = self.p
        ws, xf32, tmpg = self.gate_ws
        mub = self.bank(6)
        Rb = self.bank(7)
        p.wait("tensor", "DVE", max(xh_done, self.gate_dep))
        L = self.rw(RW_GT, 8, p=128)
        m1 = self.rw(RW_GT + 8, 1, p=128)
        m2 = self.rw(RW_GT + 9, 1, p=128)
        e1 = self.rw(RW_GT + 16, 8, p=128)
        L2 = self.rw(RW_GT + 24, 8, p=128)
        p12 = self.rw(RW_GT + 32, 2, p=128)
        ts0 = ch * 512
        for tt in range(4):
            p.wait("vector", "PE", p.cnt["PE"])
            for dc in range(NDC):
                sl = self.st(dc, ts0 + tt * 128, 128)
                mus = mub[:, tt * 128: tt * 128 + 128]
                rs = Rb[:, tt * 128: tt * 128 + 128]
                p.drain("vector")
                p.op("vector", "tensor_tensor", tmpg, sl, mus, op=ALU.add)
                p.drain("vector")
                p.op("vector", "tensor_tensor",
                     xf32[:, dc * 128:(dc + 1) * 128], tmpg, rs, op=ALU.mult,
                     sem="DVE" if dc == NDC - 1 else None)
            xfd = p.cnt["DVE"]
            p.wait("tensor", "DVE", xfd)
            for dc in range(NDC):
                p.op("tensor", "matmul", self.bank(5, 8),
                     xf32[:, dc * 128:(dc + 1) * 128],
                     self.vecs[:, VEC_COLS["gate"] + dc * 8:
                               VEC_COLS["gate"] + dc * 8 + 8],
                     start=(dc == 0), stop=(dc == NDC - 1),
                     sem="PE" if dc == NDC - 1 else None)
            mm = p.cnt["PE"]
            p.wait("vector", "PE", mm)
            p.drain("vector")
            p.op("vector", "tensor_copy", L, self.bank(5, 8))
            p.drain("vector")
            lv = p.op("vector", "tensor_copy", L2, self.bank(5, 8), sem="DVE")
            p.drain("vector")
            p.op("vector", "tensor_reduce", m1, L, axis=AX.X, op=ALU.max)
            p.drain("vector")
            p.op("vector", "tensor_scalar", e1, L, m1, None, op0=ALU.is_equal)
            p.drain("vector")
            p.op("vector", "scalar_tensor_tensor", L2, e1, -BIG, L,
                 op0=ALU.mult, op1=ALU.add)
            p.drain("vector")
            p.op("vector", "tensor_reduce", m2, L2, axis=AX.X, op=ALU.max)
            p.drain("vector")
            p.op("vector", "tensor_scalar", p12[:, 0:1], m1, m2, None,
                 op0=ALU.subtract)
            p.drain("vector")
            v = p.op("vector", "tensor_scalar", p12[:, 1:2], m2, m1, None,
                     op0=ALU.subtract, sem="DVE")
            p.wait("scalar", "DVE", v)
            p.drain("scalar")
            s = p.op("scalar", "activation", p12, p12, AF.Sigmoid, sem="ACT")
            p.wait("vector", "ACT", s)
            p.drain("vector")
            p.op("vector", "tensor_scalar", L2, L2, m2, None, op0=ALU.is_equal)
            p.drain("vector")
            p.op("vector", "tensor_scalar", e1, e1, p12[:, 0:1], None,
                 op0=ALU.mult)
            p.drain("vector")
            p.op("vector", "tensor_scalar", L2, L2, p12[:, 1:2], None,
                 op0=ALU.mult)
            p.drain("vector")
            p.op("vector", "tensor_tensor", comb[:, tt * 8:(tt + 1) * 8],
                 e1, L2, op=ALU.add, sem="DVE")
            p.drain("vector")
        return p.cnt["DVE"]

    def moe_cbro(self, ch, e, comb, cbro, gate_done):
        p = self.p
        p.wait("tensor", "DVE", gate_done)
        if hasattr(self, "cbro_last"):
            p.wait("tensor", "DVE", self.cbro_last)
        p.wait("tensor", "DVE", p.cnt["DVE"])
        for tt in range(4):
            m = p.op("tensor", "transpose", self.bank(4, 128, p=1),
                     comb[:, tt * 8 + e: tt * 8 + e + 1], self.ident_f[:, :],
                     sem="PE")
            p.wait("vector", "PE", m)
            p.drain("vector")
            v = p.op("vector", "tensor_copy", self.rw(RW_BCA + tt * 128, 128),
                     self.bank(4, 128, p=1), sem="DVE")
            p.wait("tensor", "DVE", v)
            p.op("tensor", "matmul", self.bank(5, 128, off=tt * 128),
                 self.ones_f[0:1, :], self.rw(RW_BCA + tt * 128, 128),
                 start=True, stop=True, sem="PE")
        bc = p.cnt["PE"]
        p.wait("vector", "PE", bc)
        p.drain("vector")
        v = p.op("vector", "tensor_copy", cbro[:, :], self.bank(5), sem="DVE")
        p.drain("vector")
        self.cbro_last = v
        return v

    def sub_moe(self, call):
        p, T = self.p, self.T
        with ExitStack() as pes:
            ws = WS()
            with ExitStack() as ses:
                ws.sq = self.sb(f"mo{call}_sq", [128, 1024], F32, ses)
                ws.stat = self.sb(f"mo{call}_stat", [128, 2048], F32, ses)
                self.ln_stats(ws, double=False, bro=False)
            self.p.barrier()
            with ExitStack() as ses:
                self.ffn_like(ws, ses, kind="moe", call=call)
            self.p.barrier()

    def sub_attn_b(self):
        p, T = self.p, self.T
        self._variant_a = False
        with ExitStack() as pes:
            ws = WS()
            with ExitStack() as ses:
                ws.sq = self.sb("b_sq", [128, 1024], F32, ses)
                ws.stat = self.sb("b_stat", [128, 2048], F32, ses)
                self.ln_stats(ws, double=False, bro=False)
            self.p.barrier()
            self.lat = self.sb("b_lat", [128, 2 * T], BF16, pes)
            ws.murow = self.sb("b_murow", [64, 512], BF16, pes)
            ws.Rb_sb = self.sb("b_Rb", [128, 512], F32, pes)
            with ExitStack() as ses:
                ws.wfp = self.sb("b_wfp", [128, 2048], F32, ses)
                ws.wca = self.sb("b_wca", [128, 2048], BF16, ses)
                ws.rhs2 = self.sb("b_rhs2", [128, 16], F32, ses)
                ws.rhs2b = self.sb("b_rhs2b", [128, 16], BF16, ses)
                ws.cscol = self.sb("b_cscol", [128, 1], F32, ses)
                self.prep_bias_cs(self.w_h["bWq"], "lnq_s", "lnq_b", "bq", "bq",
                                  (0, 0), 1.0, ws, want_bias=False,
                                  identity_s=True)
            self.p.barrier()
            with ExitStack() as ses:
                ws.wfp = self.sb("b_wfp2", [128, 2048], F32, ses)
                ws.cscol = self.sb("b_cscol2", [128, 1], F32, ses)
                self.kv_latent(ws, ses)
            self.p.barrier()
            self.attn_groups(ws, variant="b")
            self.p.barrier()
            self.wo_phase(ws, self.w_h["bWo"], with_bias=False)
            self.p.barrier()
        self._variant_a = True

    def kv_latent(self, ws, ses):
        p, T, NT8 = self.p, self.T, self.NT8
        wkvb = self.sb("b_wkv", [128, 8192], BF16, ses)
        wdwn = self.sb("b_wdwn", [128, 2048], BF16, ses)
        kvb = self.sb("b_kvb", [128, 1024], BF16, ses)
        xb2 = self.sb("b_xb2", [128, 4096], BF16, ses)
        cskv = self.csPool[32:33, 0:1024]
        p.op("gpsimd", "dma_start",
             wdwn[:, :].rearrange("p (dc l) -> p dc l", dc=NDC),
             self.w_h["bWdown"][:, :].rearrange("(dc p) l -> p dc l", p=128),
             sem="bup", k=16)
        dw = p.cnt["bup"]
        csacc = DB(1)
        for dcin in range(NDC):
            p.waitd("sync", csacc.reuse_dep(dcin))
            kvs = "kwA"
            p.op("sync", "dma_start", ws.wfp[:, 0:1024],
                 self.w_h["bWk"][dcin * 128:(dcin + 1) * 128, :], sem=kvs, k=16)
            d2 = p.op("sync", "dma_start", ws.wfp[:, 1024:2048],
                      self.w_h["bWv"][dcin * 128:(dcin + 1) * 128, :],
                      sem=kvs, k=16)
            p.wait("vector", kvs, d2)
            p.drain("vector")
            c = p.op("vector", "tensor_tensor",
                     wkvb[:, dcin * 1024:(dcin + 1) * 1024],
                     ws.wfp[:, 0:1024], ws.wfp[:, 1024:2048], op=ALU.add,
                     sem="DVE")
            csacc.set_done(dcin, ("DVE", c))
            p.wait("tensor", "DVE", c)
            for oc in range(NDC):
                p.op("tensor", "matmul", self.bank(oc, 1),
                     wkvb[:, dcin * 1024 + oc * 128: dcin * 1024 + oc * 128 + 128],
                     self.ones_b[:, 0:1], start=(dcin == 0), stop=(dcin == NDC - 1),
                     sem="PE" if oc == NDC - 1 else None)
        done = p.cnt["PE"]
        p.wait("vector", "PE", done)
        for oc in range(NDC):
            p.drain("vector")
            v = p.op("vector", "tensor_copy", ws.cscol[:, 0:1], self.bank(oc, 1),
                     sem="DVE")
            p.wait("tensor", "DVE", v)
            m = p.op("tensor", "transpose", self.bank(oc, 128, off=256, p=1),
                     ws.cscol[:, 0:1], self.ident_f[:, :], sem="PE")
            p.wait("vector", "PE", m)
            p.drain("vector")
            p.op("vector", "tensor_copy",
                 cskv[0:1, oc * 128: oc * 128 + 128],
                 self.bank(oc, 128, off=256, p=1), sem="DVE")
        cs_done = p.cnt["DVE"]
        p.wait("tensor", "bup", dw)
        kdb = DB(2)
        lat_ev = {}
        jk = 0
        for tq in range(NT8):
            tqs = slice(tq * 512, (tq + 1) * 512)
            sdep = self.stat_bcast_tq(ws, tq)
            p.wait("tensor", "DVE", sdep)
            if tq >= 1:
                p.wait("vector", "PE", self.kvx_pe.get(tq - 1, 0) if hasattr(self, "kvx_pe") else 0)
            p.drain("vector")
            for dc in range(NDC):
                p.op("vector", "tensor_copy", xb2[:, dc * 512:(dc + 1) * 512],
                     self.st(dc, tq * 512, 512),
                     sem="DVE" if dc == NDC - 1 else None)
            xc = p.cnt["DVE"]
            p.wait("tensor", "DVE", max(xc, cs_done))
            if tq >= 1:
                p.wait("tensor", "DVE", lat_ev.get(tq - 1, 0))
            for dcout in range(NDC):
                bk = dcout % 2
                for dcin in range(NDC):
                    p.op("tensor", "matmul", self.bank(bk),
                         wkvb[:, dcin * 1024 + dcout * 128: dcin * 1024 + dcout * 128 + 128],
                         xb2[:, dcin * 512:(dcin + 1) * 512],
                         start=(dcin == 0), stop=False)
                m = p.op("tensor", "matmul", self.bank(bk),
                         cskv[0:1, dcout * 128: dcout * 128 + 128],
                         ws.murow[32:33, :],
                         start=False, stop=True, sem="PE")
                p.wait("vector", "PE", m)
                bkv = kdb.slot(jk)
                p.waitd("vector", kdb.reuse_dep(jk))
                p.drain("vector")
                kv = p.op("vector", "tensor_tensor",
                          kvb[:, bkv * 512:(bkv + 1) * 512], self.bank(bk),
                          ws.Rb_sb[:, :], op=ALU.mult, sem="DVE")
                p.wait("tensor", "DVE", kv)
                for lc in range(2):
                    mm = p.op("tensor", "matmul", self.bank(2 + lc),
                              wdwn[:, dcout * 256 + lc * 128: dcout * 256 + lc * 128 + 128],
                              kvb[:, bkv * 512:(bkv + 1) * 512],
                              start=(dcout == 0), stop=(dcout == NDC - 1),
                              sem="PE")
                kdb.set_done(jk, ("PE", p.cnt["PE"]))
                jk += 1
            lm = p.cnt["PE"]
            p.wait("vector", "PE", lm)
            for lc in range(2):
                p.drain("vector")
                p.op("vector", "tensor_copy",
                     self.lat[:, lc * T + tq * 512: lc * T + tq * 512 + 512],
                     self.bank(2 + lc),
                     sem="DVE" if lc == 1 else None)
            lat_ev[tq] = p.cnt["DVE"]
            if not hasattr(self, "kvx_pe"):
                self.kvx_pe = {}
            self.kvx_pe[tq] = lm
        p.drain("vector")
        self.lat_done = p.cnt["DVE"]
        p.wait("tensor", "DVE", self.lat_done)

    def sub_attn_a_entry(self):
        pass



def _prep_inputs(inputs):
    g = lambda k: np.asarray(inputs[k])

    def vec(v):
        v = np.asarray(v)
        return np.ascontiguousarray(v.reshape(-1, 128).T.astype(np.float32))

    vecs = np.zeros((128, NVEC), dtype=np.float32)

    def put(name, v):
        c = VEC_COLS[name]
        vv = vec(v)
        vecs[:, c:c + vv.shape[1]] = vv

    put("lnq_s", g("L0_attn_a_lnq_s")); put("lnq_b", g("L0_attn_a_lnq_b"))
    put("lnk_s", g("L0_attn_a_lnk_s")); put("lnk_b", g("L0_attn_a_lnk_b"))
    put("lnv_s", g("L0_attn_a_lnv_s")); put("lnv_b", g("L0_attn_a_lnv_b"))
    put("lnm_s", g("L0_attn_a_lnm_s")); put("lnm_b", g("L0_attn_a_lnm_b"))
    put("bq", g("L0_attn_a_bq")); put("bk", g("L0_attn_a_bk"))
    put("bv", g("L0_attn_a_bv")); put("bo", g("L0_attn_a_bo"))
    put("b1", g("L0_attn_a_b1")); put("b2", g("L0_attn_a_b2"))
    gw = g("L0_ffn_gate_w").T.reshape(NDC, 128, NE).transpose(1, 0, 2).reshape(128, 64)
    vecs[:, VEC_COLS["gate"]:VEC_COLS["gate"] + 64] = gw

    C = np.ascontiguousarray
    f4 = np.float32
    common = {
        "vecsin": vecs,
        "aWq": C(g("L0_attn_a_Wq").astype(f4)),
        "aWk": C(g("L0_attn_a_Wk").astype(f4)),
        "aWv": C(g("L0_attn_a_Wv").astype(f4)),
        "aWo": C(g("L0_attn_a_Wo").astype(f4)),
        "aW1": C(g("L0_attn_a_W1").astype(f4)),
        "aW2": C(g("L0_attn_a_W2").astype(f4)),
        "bWq": C(g("L0_attn_b_W_q").astype(f4)),
        "bWk": C(g("L0_attn_b_W_k").astype(f4)),
        "bWv": C(g("L0_attn_b_W_v").astype(f4)),
        "bWo": C(g("L0_attn_b_W_o").astype(f4)),
        "bWdown": C(g("L0_attn_b_W_down_kv").astype(f4)),
        "bWupK": C(g("L0_attn_b_W_up_k").astype(f4)),
        "bWupV": C(g("L0_attn_b_W_up_v").astype(f4)),
        "wgT": C(g("L0_ffn_wg").transpose(0, 2, 1).astype(f4)),
        "wuT": C(g("L0_ffn_wu").transpose(0, 2, 1).astype(f4)),
        "wdT": C(g("L0_ffn_wd").transpose(0, 2, 1).astype(f4)),
    }
    x = np.asarray(inputs["L0_attn_a_x"])
    xs = [C(x[b].T.astype(f4)) for b in range(x.shape[0])]
    return common, xs


MK._variant_a = True


def kernel(**inputs):
    common, xs = _prep_inputs(inputs)
    mk = MK(T=4096)
    in_maps = [dict(common, x=xs[b]) for b in range(8)]
    res = run_bass_kernel_spmd(mk.nc, in_maps, core_ids=list(range(8)))
    outs = [np.asarray(r["out"]) for r in res.results]
    y = np.stack([o.T for o in outs], axis=0)
    return np.ascontiguousarray(y.astype(np.float32))


# version 09
# speedup vs baseline: 1.0364x; 1.0364x over previous
import math
import numpy as np
from contextlib import ExitStack

import concourse.bass as bass
import concourse.mybir as mybir
from concourse.bass_utils import run_bass_kernel_spmd

F32 = mybir.dt.float32
BF16 = mybir.dt.bfloat16
AF = mybir.ActivationFunctionType
ALU = mybir.AluOpType
AX = mybir.AxisListType

D = 1024
NDC = 8
HEADS = 16
HD = 64
NG = 8
DL = 256
NE = 8
IMOE = 4096
IMLP = 2048
LAM = 0.01
EPS = 1e-5
INV32 = 1.0 / 32.0
BIG = 1.0e30

VEC_COLS = {
    "lnq_s": 0, "lnq_b": 8, "lnk_s": 16, "lnk_b": 24,
    "lnv_s": 32, "lnv_b": 40, "lnm_s": 48, "lnm_b": 56,
    "bq": 64, "bk": 72, "bv": 80, "bo": 88,
    "b1": 96, "b2": 112, "gate": 120,
}
NVEC = 192
BA = {"bq": 0, "bk": 8, "bv": 16, "bo": 24, "b1": 32}

RW_RCP = 0
RW_GT = 64
RW_TMPG = 384
RW_BCA = 512
RW_I128C = 512
RW_I128R = 640
RW_I512 = 512
RW_QC = 0
RW_KC = 16
RW_MC = 32
RW_MT = 48


class Prog:
    ENGS = ("sync", "gpsimd", "tensor", "vector", "scalar")

    def __init__(self, nc, es):
        self.nc = nc
        self.es = es
        self.q = {e: [] for e in self.ENGS}
        self.sems = {}
        self.cnt = {}
        for name in ("dG", "dS", "PE", "ACT", "DVE", "POOL"):
            self.sem(name)

    def sem(self, name):
        if name not in self.sems:
            self.sems[name] = self.es.enter_context(self.nc.semaphore(name))
            self.cnt[name] = 0
        return self.sems[name]

    def op(self, eng, meth, *args, sem=None, k=1, **kw):
        s = self.sem(sem) if sem else None

        def ap(x):
            if isinstance(x, (bass.SBTensorHandle, bass.PSumTensorHandle,
                              bass.DRamTensorHandle)):
                return x[:, :] if len(x.shape) == 2 else x[:]
            return x
        args = tuple(ap(x) for x in args)
        kw = {kk: ap(vv) for kk, vv in kw.items()}

        def f(e, meth=meth, args=args, kw=kw, s=s, k=k):
            r = getattr(e, meth)(*args, **kw)
            if s is not None:
                r.then_inc(s, k)
        self.q[eng].append(f)
        if sem:
            self.cnt[sem] += k
            return self.cnt[sem]
        return None

    def wait(self, eng, sem, val):
        if not val or val <= 0:
            return
        s = self.sem(sem)
        self.q[eng].append(lambda e, s=s, val=val: e.wait_ge(s, val))

    def waitd(self, eng, dep):
        if dep:
            self.wait(eng, dep[0], dep[1])

    def drain(self, eng):
        self.q[eng].append(lambda e: e.drain())

    MILE = {"tensor": "PE", "vector": "DVE", "scalar": "ACT",
            "gpsimd": "POOL"}

    def fence(self, eng):
        name = self.MILE[eng]
        self.drain(eng)
        s = self.sems[name]
        self.q[eng].append(lambda e, s=s: e.sem_inc(s, 1))
        self.cnt[name] += 1
        return self.cnt[name]

    BAR = {"tensor": "barT", "vector": "barV", "scalar": "barS",
           "gpsimd": "barG"}

    def barrier(self):
        for eng, bs in self.BAR.items():
            if eng == "gpsimd":
                continue
            self.sem(bs)
            self.drain(eng)
            s = self.sems[bs]
            self.q[eng].append(lambda e, s=s: e.sem_inc(s, 1))
            self.cnt[bs] += 1
        snap = [(n, c) for n, c in self.cnt.items()
                if c > 0 and not n.startswith(("tci", "tco"))]
        for eng in self.ENGS:
            for n, c in snap:
                self.wait(eng, n, c)

    def flush(self):
        with self.nc.Block() as blk:
            for eng in self.ENGS:
                if not self.q[eng]:
                    continue
                ops = self.q[eng]

                def run(e, ops=ops):
                    for f in ops:
                        f(e)
                getattr(blk, eng)(run)
        self.q = {e: [] for e in self.ENGS}


class DB:

    def __init__(self, n=2):
        self.n = n
        self.free = [None] * n

    def slot(self, i):
        return i % self.n

    def reuse_dep(self, i):
        return self.free[i % self.n]

    def set_done(self, i, dep):
        self.free[i % self.n] = dep


class WS:
    pass


class MK:
    def __init__(self, T=4096, stop_after=None, debug_taps=()):
        self.T = T
        self.NT8 = T // 512
        self.NTT = T // 128
        self.NW = T // 512
        self.NKC = T // 128
        self.stop_after = stop_after
        self.debug_taps = set(debug_taps)
        self.debug_out = {}
        self.nc = bass.Bass("TRN2", target_bir_lowering=False)
        self.es = ExitStack()
        self.build()

    def sb(self, name, shape, dtype, es=None):
        return (es or self.es).enter_context(self.nc.sbuf_tensor(name, shape, dtype))

    def bank(self, b, w=512, off=0, p=128, p0=0):
        return self.P8[p0:p0 + p, b * 512 + off: b * 512 + off + w]

    def vcol(self, name, i=0, n=1):
        c = VEC_COLS[name] + i
        return self.vecs[:, c:c + n]

    def bcol(self, name, i=0, n=1):
        c = BA[name] + i
        return self.biasA[:, c:c + n]

    def st(self, dc, c0=0, n=None):
        if n is None:
            n = self.T
        return self.stream[:, dc * self.T + c0: dc * self.T + c0 + n]

    def rw(self, c0, n, p0=0, p=1):
        return self.rows[p0:p0 + p, c0:c0 + n]

    def cw(self, c0, n, p=1):
        return self.cells[0:p, c0:c0 + n]

    def tap(self, name, ap):
        if name not in self.debug_taps:
            return
        p = self.p
        sh = list(ap.shape)
        t = self.nc.dram_tensor("dbg_" + name, sh, ap.dtype, kind="ExternalOutput")
        self.debug_out[name] = "dbg_" + name
        for s in ("DVE", "ACT", "PE", "POOL"):
            p.wait("sync", s, p.cnt[s])
        p.op("sync", "dma_start", t[:, :] if len(sh) == 2 else t[:], ap,
             sem="dS", k=16)
        p.wait("sync", "dS", p.cnt["dS"])
        s = p.sem("TAP")
        p.q["sync"].append(lambda e, s=s: e.sem_inc(s, 1))
        p.cnt["TAP"] += 1
        for e in ("tensor", "vector", "scalar", "gpsimd"):
            p.wait(e, "TAP", p.cnt["TAP"])

    def bcast_cols(self, src, ncols, b, tb=7):
        p = self.p
        for t in range(ncols):
            m = p.op("tensor", "transpose", self.bank(tb, 128, p=1),
                     src[:, t:t + 1], self.ident_f[:, :], sem="PE")
            p.wait("vector", "PE", m)
            p.drain("vector")
            v = p.op("vector", "tensor_copy", self.rw(RW_BCA + t * 128, 128),
                     self.bank(tb, 128, p=1), sem="DVE")
            p.wait("tensor", "DVE", v)
            p.op("tensor", "matmul", self.bank(b, 128, off=t * 128),
                 self.ones_f[0:1, :], self.rw(RW_BCA + t * 128, 128),
                 start=True, stop=True, sem="PE")
        return p.cnt["PE"]

    def build(self):
        nc, es = self.nc, self.es
        T = self.T

        self.x_h = nc.dram_tensor("x", [D, T], F32, kind="ExternalInput")
        self.out_h = nc.dram_tensor("out", [D, T], F32, kind="ExternalOutput")
        self.vecs_h = nc.dram_tensor("vecsin", [128, NVEC], F32, kind="ExternalInput")
        wnames = {
            "aWq": [D, D], "aWk": [D, D], "aWv": [D, D], "aWo": [D, D],
            "aW1": [D, IMLP], "aW2": [IMLP, D],
            "bWq": [D, D], "bWk": [D, D], "bWv": [D, D], "bWo": [D, D],
            "bWdown": [D, DL], "bWupK": [DL, D], "bWupV": [DL, D],
            "wgT": [NE, D, IMOE], "wuT": [NE, D, IMOE], "wdT": [NE, IMOE, D],
        }
        self.w_h = {k: nc.dram_tensor(k, s, F32, kind="ExternalInput")
                    for k, s in wnames.items()}
        self.attscr = nc.dram_tensor("attscr", [D, T], BF16)
        self.wgB = nc.dram_tensor("wgB", [NE, D, IMOE], BF16)
        self.wuB = nc.dram_tensor("wuB", [NE, D, IMOE], BF16)
        self.wdB = nc.dram_tensor("wdB", [NE, IMOE, D], BF16)

        self.stream = self.sb("stream", [128, NDC * T], F32)
        self.vecs = self.sb("vecs", [128, NVEC], F32)
        self.ones_f = self.sb("ones_f", [128, 128], F32)
        self.ones_b = self.sb("ones_b", [128, 8], BF16)
        self.ident_f = self.sb("ident_f", [128, 128], F32)
        self.csPool = self.sb("csPool", [128, 1024], BF16)
        self.rows = self.sb("rows", [128, 1024], F32)
        self.biasA = self.sb("biasA", [128, 48], F32)
        self.R_tok = self.sb("R_tok", [128, 32], F32)
        self.mu_tok = self.sb("mu_tok", [128, 32], F32)
        self.r1_tok = self.sb("r1_tok", [128, 32], F32)
        self.Mneg = self.sb("Mneg", [128, 16], F32)
        self.loc2 = self.sb("loc2", [128, 2], F32)
        self.cells = self.sb("cells", [128, 52], F32)
        self.maskc = self.sb("maskc", [128, 1], F32)
        self.vpm = self.sb("vpm", [128, 64], F32)
        self.constA = self.sb("constA", [128, 2], F32)
        self.P8 = es.enter_context(nc.psum_tensor("P8", [128, 4096], F32))

        self.p = Prog(nc, es)

        self.phase_load()
        self.phase_transcode()
        done = self.stop_after == "load"
        if not done:
            self.sub_attn_a()
            done = self.stop_after == "attn_a"
        self.p.barrier()
        for eng in Prog.ENGS:
            for dep in self.transcode_done:
                self.p.waitd(eng, dep)
        self.stg_es.close()
        if not done:
            self.sub_moe(0)
            done = self.stop_after == "moe1"
        if not done:
            self.sub_attn_b()
            done = self.stop_after == "attn_b"
        if not done:
            self.sub_moe(1)

        self.phase_out()
        self.p.flush()
        self.es.close()

    def phase_load(self):
        p, T = self.p, self.T
        for dc in range(NDC):
            p.op("sync", "dma_start", self.st(dc),
                 self.x_h[dc * 128:(dc + 1) * 128, :], sem="dS", k=16)
        dv = p.op("sync", "dma_start", self.vecs[:, :], self.vecs_h[:, :],
                  sem="dS", k=16)
        p.op("vector", "memset", self.ones_f[:, :], 1.0)
        p.op("vector", "memset", self.ones_b[:, :], 1.0)
        p.op("vector", "memset", self.constA[:, 0:1], EPS)
        p.op("vector", "memset", self.loc2[:, :], 0.0)
        p.op("vector", "drain")
        p.op("vector", "memset", self.loc2[0:64, 0:1], 1.0)
        p.op("vector", "drain")
        p.op("vector", "memset", self.loc2[64:128, 1:2], 1.0)
        p.op("vector", "drain")
        p.op("vector", "memset", self.constA[:, 1:2], 1e-38, sem="DVE")
        self.const_dep = p.cnt["DVE"]
        self.gate_dep = 0
        p.op("gpsimd", "iota", self.rw(RW_I128C, 128, p=128), [[1, 128]],
             channel_multiplier=0, allow_small_or_imprecise_dtypes=True)
        p.drain("gpsimd")
        p.op("gpsimd", "iota", self.rw(RW_I128R, 128, p=128), [[0, 128]],
             channel_multiplier=1, allow_small_or_imprecise_dtypes=True, sem="POOL")
        g1 = p.cnt["POOL"]
        p.wait("vector", "POOL", g1)
        p.drain("vector")
        p.op("vector", "tensor_tensor", self.ident_f[:, :],
             self.rw(RW_I128C, 128, p=128), self.rw(RW_I128R, 128, p=128),
             op=ALU.is_equal)
        p.drain("vector")
        p.op("vector", "tensor_scalar", self.maskc[:, :],
             self.rw(RW_I128R, 1, p=128), 127.0, -999.0,
             op0=ALU.is_equal, op1=ALU.mult, sem="DVE")
        self.ident_dep = p.cnt["DVE"]
        p.wait("gpsimd", "DVE", self.ident_dep)
        p.drain("gpsimd")
        p.op("gpsimd", "iota", self.rw(RW_BCA, 32, p=128), [[128, 32]],
             channel_multiplier=1, allow_small_or_imprecise_dtypes=True, sem="POOL")
        g2 = p.cnt["POOL"]
        p.wait("scalar", "POOL", g2)
        p.drain("scalar")
        p.op("scalar", "activation", self.vpm[:, 0:32], self.rw(RW_BCA, 32, p=128),
             AF.Exp, scale=LAM)
        p.drain("scalar")
        p.op("scalar", "activation", self.vpm[:, 32:64], self.rw(RW_BCA, 32, p=128),
             AF.Exp, scale=-LAM, sem="ACT")
        a2 = p.cnt["ACT"]
        p.wait("vector", "ACT", a2)
        p.drain("vector")
        p.op("vector", "tensor_scalar", self.vpm[:, 0:64], self.vpm[:, 0:64],
             INV32, None, op0=ALU.mult, sem="DVE")
        self.vpm_dep = p.cnt["DVE"]
        self.load_dS = p.cnt["dS"]
        self.load_ACT = p.cnt["ACT"]

    def phase_transcode(self):
        p = self.p
        self.stg_es = ExitStack()
        stg = self.sb("stg", [128, 2048], BF16, self.stg_es)
        db = DB(2)
        jobs = []
        for e in range(NE):
            for src, dst in ((self.w_h["wgT"], self.wgB), (self.w_h["wuT"], self.wuB)):
                for dc in range(NDC):
                    for hh in range(4):
                        jobs.append((src[e, dc * 128:(dc + 1) * 128,
                                         hh * 1024:(hh + 1) * 1024],
                                     dst[e, dc * 128:(dc + 1) * 128,
                                         hh * 1024:(hh + 1) * 1024], None))
            for r in range(4 * NDC):
                s = self.w_h["wdT"][e, r * 128:(r + 1) * 128, :].rearrange(
                    "(r p) d -> p r d", p=128)
                d = self.wdB[e, r * 128:(r + 1) * 128, :].rearrange(
                    "(r p) d -> p r d", p=128)
                jobs.append((s, d, 1))
        for i, (s, d, r3) in enumerate(jobs):
            b = db.slot(i)
            tile = stg[:, b * 1024:(b + 1) * 1024]
            if r3:
                tile = tile.rearrange("p (r d) -> p r d", r=r3)
            b2 = "AB"[b]
            p.waitd("gpsimd", db.reuse_dep(i))
            gi = p.op("gpsimd", "dma_start", tile, s, sem="tci" + b2, k=16)
            p.wait("gpsimd", "tci" + b2, gi)
            c = p.op("gpsimd", "dma_start", d, tile, sem="tco" + b2, k=16)
            db.set_done(i, ("tco" + b2, c))
        self.transcode_done = [("tcoA", p.cnt["tcoA"]), ("tcoB", p.cnt["tcoB"])]

    def phase_out(self):
        p = self.p
        for s in ("DVE", "ACT", "PE", "POOL"):
            p.wait("sync", s, p.cnt[s])
        for dc in range(NDC):
            p.op("sync", "dma_start", self.out_h[dc * 128:(dc + 1) * 128, :],
                 self.st(dc), sem="dS", k=16)
        p.wait("sync", "dS", p.cnt["dS"])

    def ln_stats(self, ws, double, bro=True):
        p, NT8 = self.p, self.NT8
        if not hasattr(self, "tokstat_read"):
            self.tokstat_read = 0
        p.wait("tensor", "dS", self.load_dS)
        p.wait("tensor", "DVE", self.const_dep)
        p.wait("scalar", "dS", self.load_dS)
        p.wait("vector", "dS", self.load_dS)
        sq = ws.sq
        sc = ws.stat
        mus, var, rstd, tmp = (sc[:, i * 512:(i + 1) * 512] for i in range(4))
        evac_done = {}
        sqdb = DB(2)
        j = 0
        for t8 in range(NT8):
            bS = (t8 % 2) * 2
            bQ = bS + 1
            ts8 = slice(t8 * 512, (t8 + 1) * 512)
            if t8 >= 2:
                p.wait("tensor", "DVE", evac_done[t8 - 2])
            for dc in range(NDC):
                xs = self.st(dc, t8 * 512, 512)
                p.op("tensor", "matmul", self.bank(bS), self.ones_f[:, :], xs,
                     start=(dc == 0), stop=(dc == NDC - 1),
                     sem="PE" if dc == NDC - 1 else None)
                b = sqdb.slot(j)
                sqb = sq[:, b * 512:(b + 1) * 512]
                p.waitd("scalar", sqdb.reuse_dep(j))
                p.drain("scalar")
                a = p.op("scalar", "activation", sqb, xs, AF.Square, sem="ACT")
                p.wait("tensor", "ACT", a)
                m = p.op("tensor", "matmul", self.bank(bQ), self.ones_f[:, :], sqb,
                         start=(dc == 0), stop=(dc == NDC - 1), sem="PE")
                sqdb.set_done(j, ("PE", m))
                j += 1
            mm_done = p.cnt["PE"]
            p.wait("vector", "PE", mm_done)
            p.drain("vector")
            p.op("vector", "tensor_scalar", mus, self.bank(bS), 1.0 / D, None,
                 op0=ALU.mult)
            p.drain("vector")
            v = p.op("vector", "tensor_scalar", tmp, mus, -1.0, None,
                     op0=ALU.mult, sem="DVE")
            p.wait("scalar", "DVE", v)
            p.drain("scalar")
            if bro:
                p.op("scalar", "activation", ws.mu_bro[:, ts8], tmp, AF.Copy)
            p.drain("scalar")
            p.wait("tensor", "DVE", v)
            p.wait("tensor", "DVE", self.tokstat_read)
            for tt in range(4):
                m = p.op("tensor", "transpose", self.bank(6, 128, off=tt * 128),
                         tmp[:, tt * 128:(tt + 1) * 128],
                         self.ident_f[:, :], sem="PE" if tt == 3 else None)
            p.wait("vector", "PE", p.cnt["PE"])
            p.drain("vector")
            cpv = p.op("vector", "tensor_copy",
                       self.mu_tok[:, t8 * 4: t8 * 4 + 4],
                       self.bank(6).rearrange("p (c x) -> p c x", c=4)[:, :, 0:1]
                       .rearrange("p c one -> p (c one)"),
                       sem="DVE")
            self.tokstat_read = cpv
            p.wait("scalar", "PE", p.cnt["PE"])
            p.drain("scalar")
            a = p.op("scalar", "activation", tmp, mus, AF.Square, sem="ACT")
            p.wait("vector", "ACT", a)
            p.drain("vector")
            p.op("vector", "scalar_tensor_tensor", var, self.bank(bQ), 1.0 / D,
                 tmp, op0=ALU.mult, op1=ALU.subtract, sem="DVE")
            p.wait("scalar", "DVE", p.cnt["DVE"])
            p.drain("scalar")
            p.wait("scalar", "DVE", self.const_dep)
            p.op("scalar", "activation", tmp, var, AF.Ln, bias=self.constA[:, 0:1])
            p.drain("scalar")
            a = p.op("scalar", "activation", rstd, tmp, AF.Exp, scale=-0.5,
                     sem="ACT")
            if double:
                p.wait("vector", "ACT", a)
                p.drain("vector")
                p.op("vector", "tensor_tensor", tmp, var, rstd, op=ALU.mult)
                p.drain("vector")
                v = p.op("vector", "tensor_tensor", tmp, tmp, rstd, op=ALU.mult,
                         sem="DVE")
                p.wait("scalar", "DVE", v)
                p.drain("scalar")
                p.op("scalar", "activation", tmp, tmp, AF.Ln, bias=self.constA[:, 0:1])
                p.drain("scalar")
                a2 = p.op("scalar", "activation", tmp, tmp, AF.Exp, scale=-0.5,
                          sem="ACT")
                p.wait("vector", "ACT", a2)
                p.drain("vector")
                p.op("vector", "tensor_tensor", var, rstd, tmp, op=ALU.mult,
                     sem="DVE")
                Rsrc = var
                p.wait("tensor", "DVE", p.cnt["DVE"])
                for tt in range(4):
                    p.op("tensor", "transpose", self.bank(6, 128, off=tt * 128),
                         rstd[:, tt * 128:(tt + 1) * 128],
                         self.ident_f[:, :], sem="PE" if tt == 3 else None)
                p.wait("vector", "PE", p.cnt["PE"])
                p.drain("vector")
                cpv = p.op("vector", "tensor_copy",
                           self.r1_tok[:, t8 * 4: t8 * 4 + 4],
                           self.bank(6).rearrange("p (c x) -> p c x", c=4)[:, :, 0:1]
                           .rearrange("p c one -> p (c one)"),
                           sem="DVE")
                self.tokstat_read = cpv
            else:
                Rsrc = rstd
            p.wait("vector", "ACT", p.cnt["ACT"])
            p.drain("vector")
            if bro:
                p.op("vector", "tensor_copy", ws.R_bro[:, ts8], Rsrc)
            p.drain("vector")
            rdone = p.op("vector", "tensor_copy", Rsrc[:, 0:1], Rsrc[:, 0:1],
                         sem="DVE")
            p.wait("tensor", "DVE", rdone)
            p.wait("tensor", "DVE", self.tokstat_read)
            for tt in range(4):
                p.op("tensor", "transpose", self.bank(6, 128, off=tt * 128),
                     Rsrc[:, tt * 128:(tt + 1) * 128],
                     self.ident_f[:, :], sem="PE" if tt == 3 else None)
            p.wait("vector", "PE", p.cnt["PE"])
            p.drain("vector")
            cpv = p.op("vector", "tensor_copy",
                       self.R_tok[:, t8 * 4: t8 * 4 + 4],
                       self.bank(6).rearrange("p (c x) -> p c x", c=4)[:, :, 0:1]
                       .rearrange("p c one -> p (c one)"),
                       sem="DVE")
            self.tokstat_read = cpv
            evac_done[t8] = p.cnt["DVE"]
        p.drain("vector")
        p.drain("scalar")
        self.stats_done = p.cnt["DVE"]

    def prep_bias_cs(self, w_hbm, s_name, b_name, badd_name, bias_name, cs_slot,
                     scale, ws, want_bias=True, identity_s=False):
        p = self.p
        wfp, wca, rhs2, rhs2b, cscol = ws.wfp, ws.wca, ws.rhs2, ws.rhs2b, ws.cscol
        p.drain("vector")
        for dc in range(NDC):
            if identity_s:
                p.op("vector", "tensor_copy", rhs2[:, 2 * dc:2 * dc + 1],
                     self.ones_f[:, 0:1])
            else:
                p.op("vector", "tensor_copy", rhs2[:, 2 * dc:2 * dc + 1],
                     self.vecs[:, VEC_COLS[s_name] + dc:VEC_COLS[s_name] + dc + 1])
            p.op("vector", "tensor_copy", rhs2[:, 2 * dc + 1:2 * dc + 2],
                 self.vecs[:, VEC_COLS[b_name] + dc:VEC_COLS[b_name] + dc + 1]
                 if not identity_s else self.ones_f[:, 0:1])
        p.waitd("vector", getattr(self, "wca_free", None))
        p.drain("vector")
        rdy = p.op("vector", "tensor_copy", rhs2b[:, :], rhs2[:, :], sem="DVE")
        if not hasattr(self, "pb_wdb"):
            self.pb_wdb = DB(2)
        wdb = self.pb_wdb
        for dc in range(NDC):
            b = wdb.slot(dc)
            p.waitd("sync", wdb.reuse_dep(dc))
            d = p.op("sync", "dma_start", wfp[:, b * 1024:(b + 1) * 1024],
                     w_hbm[dc * 128:(dc + 1) * 128, :], sem="pb" + "AB"[b], k=16)
            p.wait("vector", "pb" + "AB"[b], d)
            p.drain("vector")
            p.waitd("vector", getattr(self, "wca_free", None))
            c = p.op("vector", "tensor_copy", wca[:, b * 1024:(b + 1) * 1024],
                     wfp[:, b * 1024:(b + 1) * 1024], sem="DVE")
            wdb.set_done(dc, ("DVE", c))
            p.wait("tensor", "DVE", max(c, rdy))
            for oc in range(NDC):
                m = p.op("tensor", "matmul", self.bank(oc, 2),
                         wca[:, b * 1024 + oc * 128: b * 1024 + oc * 128 + 128],
                         rhs2b[:, 2 * dc:2 * dc + 2],
                         start=(dc == 0), stop=(dc == NDC - 1),
                         sem="PE" if oc == NDC - 1 else None)
            if dc >= 1:
                self.wca_free = ("PE", p.fence("tensor"))
        done = p.fence("tensor")
        self.wca_free = ("PE", done)
        p.wait("vector", "PE", done)
        row, col0 = cs_slot
        for oc in range(NDC):
            if want_bias:
                p.drain("vector")
                p.op("vector", "tensor_scalar",
                     self.biasA[:, BA[bias_name] + oc:BA[bias_name] + oc + 1],
                     self.bank(oc, 1, off=1),
                     self.vecs[:, VEC_COLS[badd_name] + oc:VEC_COLS[badd_name] + oc + 1],
                     scale, op0=ALU.add, op1=ALU.mult, sem="DVE")
            p.drain("vector")
            v = p.op("vector", "tensor_copy", cscol[:, 0:1], self.bank(oc, 1),
                     sem="DVE")
            p.wait("tensor", "DVE", v)
            m = p.op("tensor", "transpose", self.bank(oc, 128, off=256, p=1),
                     cscol[:, 0:1], self.ident_f[:, :], sem="PE")
            p.wait("vector", "PE", m)
            p.drain("vector")
            p.op("vector", "tensor_scalar",
                 self.csPool[row:row + 1, col0 + oc * 128: col0 + oc * 128 + 128],
                 self.bank(oc, 128, off=256, p=1), scale, None,
                 op0=ALU.mult, sem="DVE")
        p.drain("vector")
        return p.cnt["DVE"]

    def sub_attn_a(self):
        p, T = self.p, self.T
        with ExitStack() as pes:
            ws = WS()
            ws.mu_bro = self.sb("a_mu", [128, T], BF16, pes)
            ws.R_bro = self.sb("a_R", [128, T], BF16, pes)
            with ExitStack() as ses:
                ws.sq = self.sb("a_sq", [128, 1024], F32, ses)
                ws.stat = self.sb("a_stat", [128, 2048], F32, ses)
                self.ln_stats(ws, double=True)
            self.p.barrier()
            with ExitStack() as ses:
                ws.wfp = self.sb("a_wfp", [128, 2048], F32, ses)
                ws.wca = self.sb("a_wca", [128, 2048], BF16, ses)
                ws.rhs2 = self.sb("a_rhs2", [128, 16], F32, ses)
                ws.rhs2b = self.sb("a_rhs2b", [128, 16], BF16, ses)
                ws.cscol = self.sb("a_cscol", [128, 1], F32, ses)
                self.prep_bias_cs(self.w_h["aWq"], "lnq_s", "lnq_b", "bq", "bq",
                                  (0, 0), INV32, ws)
                self.prep_bias_cs(self.w_h["aWk"], "lnk_s", "lnk_b", "bk", "bk",
                                  (32, 0), 1.0, ws)
                self.prep_bias_cs(self.w_h["aWv"], "lnv_s", "lnv_b", "bv", "bv",
                                  (64, 0), 1.0, ws)
            self.p.barrier()
            self.attn_groups(ws, variant="a")
            self.p.barrier()
            self.y0_add(ws)
            self.p.barrier()
            self.wo_phase(ws, self.w_h["aWo"], with_bias=True)
            self.p.barrier()
            with ExitStack() as ses:
                ws.sq = self.sb("a_sq2", [128, 1024], F32, ses)
                ws.stat = self.sb("a_stat2", [128, 2048], F32, ses)
                self.ln_stats(ws, double=False)
            self.p.barrier()
            with ExitStack() as ses:
                self.ffn_like(ws, ses, kind="mlp")
            self.p.barrier()

    def attn_groups(self, ws, variant):
        p, T, NT8, NW = self.p, self.T, self.NT8, self.NW
        a = variant == "a"
        V = variant
        with ExitStack() as ses:
            Qg = self.sb(V + "t_Qg", [128, T], BF16, ses)
            Kg = self.sb(V + "t_Kg", [128, T], BF16, ses)
            Vb = self.sb(V + "t_Vb", [128, 130 * self.NKC], BF16, ses)
            PT = self.sb(V + "t_PT", [128, max(self.NKC * 256, 7168)], BF16, ses)
            xbf = PT[:, 0:4096]
            qsc = self.sb(V + "t_qsc", [128, 512], F32, ses)
            atts = qsc[:, 0:512]
            attst = self.sb(V + "t_attst", [128, 1024], BF16, ses)
            sqn = attst[:, :].bitcast(F32)
            if a:
                wpg = PT[:, 4096:7168]
                wfpg = self.sb("at_wfpg", [128, 1024], F32, ses)
                self.pt_free = 0
                qtw = self.sb("at_qpad", [128, 1024], BF16, ses)
                qhw = Ew = wupg = None
            else:
                wpg = PT[:, 4096:6144]
                wfpg = None
                self.pt_free = 0
                qtw = self.sb("bt_qtw", [128, 1024], BF16, ses)
                qhw = self.sb("bt_qhw", [128, 1024], BF16, ses)
                Ew = None
                wupg = PT[:, 6144:6656]
                self.E0pb = self.sb("bt_E0p", [128, 512], BF16, ses)
                self.E0mb = self.sb("bt_E0m", [128, 512], BF16, ses)
                self.CM = self.sb("bt_CM", [128, 896], BF16, ses)
                self.build_decay_tiles()
                self.tap("CM", self.CM[:, :])
                self.tap("E0p", self.E0pb[:, :])
                self.tap("E0m", self.E0mb[:, :])
            for kc in range(self.NKC):
                p.op("vector", "memset", Vb[:, kc * 130 + 64: kc * 130 + 65], 1.0)
                p.op("vector", "memset", Vb[:, kc * 130 + 129: kc * 130 + 130], 1.0)
            self.vb_ones = p.fence("vector")
            self.nsqn = sqn
            self.quad_hist = []
            self.attb_free = None
            self.attst_dma = {}
            self.defer_norm = None
            self.attb_free = {}
            self.qw_dep = None
            self.qsc_act_free = 0
            self.qtw_pe = 0
            self.group_core_pe = 0
            self.group_core_dve = 0
            for g in range(NG):
                self.proj_group(ws, g, variant, Qg, Kg, Vb, xbf, qsc, sqn,
                                wpg, wfpg, wupg)
                mdep = self.finalize_M(g)
                for loc in range(2):
                    h = 2 * g + loc
                    for w in range(NW):
                        self.attn_core(ws, g, loc, h, w, variant, Qg, Kg,
                                       Vb, PT, atts, attst, qtw, qhw, Ew, mdep)
            p.drain("vector")
            p.drain("scalar")

    def stat_bcast_tq(self, ws, tq):
        p = self.p
        p.wait("tensor", "DVE", p.cnt["DVE"])
        p.wait("tensor", "DVE", self.stats_done)
        bc = self.bcast_cols(self.mu_tok[:, tq * 4:(tq + 1) * 4], 4, 6, tb=7)
        p.wait("vector", "PE", bc)
        p.drain("vector")
        mv = p.op("vector", "tensor_copy", ws.murow[:, :],
                  self.bank(6, 512, p=64), sem="DVE")
        p.wait("tensor", "DVE", mv)
        bc = self.bcast_cols(self.R_tok[:, tq * 4:(tq + 1) * 4], 4, 6, tb=7)
        p.wait("vector", "PE", bc)
        p.drain("vector")
        rv = p.op("vector", "tensor_copy", ws.Rb_sb[:, :], self.bank(6),
                  sem="DVE")
        return rv

    def build_decay_tiles(self):
        p = self.p
        p.barrier()
        cmstg = self.rows[0:128, 0:896]
        p.op("gpsimd", "iota", cmstg, [[1, 896]], base=-384,
             channel_multiplier=0, allow_small_or_imprecise_dtypes=True)
        p.drain("gpsimd")
        p.op("gpsimd", "iota", self.cw(50, 1, p=128), [[0, 1]],
             channel_multiplier=1, allow_small_or_imprecise_dtypes=True,
             sem="POOL")
        gd = p.cnt["POOL"]
        p.wait("vector", "POOL", gd)
        p.drain("vector")
        v = p.op("vector", "tensor_scalar", cmstg, cmstg,
                 self.cw(50, 1, p=128), 0.0,
                 op0=ALU.subtract, op1=ALU.min, sem="DVE")
        p.wait("scalar", "DVE", v)
        p.drain("scalar")
        a1 = p.op("scalar", "activation", self.CM[:, :], cmstg, AF.Exp,
                  scale=2.0 * LAM, sem="ACT")
        p.wait("gpsimd", "ACT", a1)
        p.drain("gpsimd")
        p.op("gpsimd", "iota", self.rw(RW_BCA, 512, p=128), [[1, 512]],
             channel_multiplier=0, allow_small_or_imprecise_dtypes=True,
             sem="POOL")
        g2 = p.cnt["POOL"]
        p.wait("scalar", "POOL", g2)
        p.drain("scalar")
        p.op("scalar", "activation", self.E0pb[:, :], self.rw(RW_BCA, 512, p=128),
             AF.Exp, scale=LAM)
        p.drain("scalar")
        ae = p.op("scalar", "activation", self.E0mb[:, :],
                  self.rw(RW_BCA, 512, p=128), AF.Exp, scale=-LAM, sem="ACT")
        self.decay_dep = (p.cnt["ACT"], v)
        p.wait("vector", "ACT", p.cnt["ACT"])
        p.drain("vector")

    def proj_group(self, ws, g, variant, Qg, Kg, Vb, xbf, qsc, sqn,
                   wpg, wfpg, wupg):
        p, T, NT8 = self.p, self.T, self.NT8
        a = variant == "a"
        if g > 0:
            p.wait("vector", "PE", self.group_core_pe)
            p.wait("vector", "DVE", self.group_core_dve)
            p.wait("scalar", "PE", self.group_core_pe)
            p.wait("tensor", "ACT", p.cnt["ACT"])
            p.wait("tensor", "DVE", self.group_core_dve)
            p.wait("sync", "PE", self.group_core_pe)
            p.wait("sync", "ACT", p.cnt["ACT"])
            p.wait("gpsimd", "PE", self.group_core_pe)
            p.wait("gpsimd", "ACT", p.cnt["ACT"])
            p.waitd("scalar", self.attst_dma.get(0))
            p.waitd("scalar", self.attst_dma.get(1))
        if a:
            specs = [(self.w_h["aWq"], 0, "lnq_s", INV32),
                     (self.w_h["aWk"], 1024, "lnk_s", 1.0),
                     (self.w_h["aWv"], 2048, "lnv_s", 1.0)]
            fold_dep = 0
            for wi, (wh, c0, s_name, scale) in enumerate(specs):
                d = p.op("sync", "dma_start",
                         wfpg[:, :].rearrange("p (dc c) -> p dc c", dc=NDC),
                         wh[:, g * 128:(g + 1) * 128]
                         .rearrange("(dc p) c -> p dc c", p=128), sem="wfa", k=16)
                p.wait("vector", "wfa", d)
                if g > 0 and wi == 0:
                    p.wait("vector", "PE", self.group_core_pe)
                p.drain("vector")
                for dc in range(NDC):
                    p.op("vector", "tensor_scalar",
                         wpg[:, c0 + dc * 128: c0 + dc * 128 + 128],
                         wfpg[:, dc * 128:(dc + 1) * 128],
                         self.vecs[:, VEC_COLS[s_name] + dc:VEC_COLS[s_name] + dc + 1],
                         scale, op0=ALU.mult, op1=ALU.mult,
                         sem="DVE" if dc == NDC - 1 else None)
                fold_dep = p.cnt["DVE"]
                if wi < 2:
                    p.wait("sync", "DVE", fold_dep)
            wq_dep = ("DVE", fold_dep)
        else:
            half = g % 2
            if g >= 2:
                p.wait("gpsimd", "PE", self.group_core_pe)
            d = p.op("gpsimd", "dma_start",
                     wpg[:, half * 1024:(half + 1) * 1024]
                     .rearrange("p (dc c) -> p dc c", dc=NDC),
                     self.w_h["bWq"][:, g * 128:(g + 1) * 128]
                     .rearrange("(dc p) c -> p dc c", p=128),
                     sem="bq" + "AB"[half], k=16)
            for wi, wh in enumerate((self.w_h["bWupK"], self.w_h["bWupV"])):
                p.op("gpsimd", "dma_start",
                     wupg[:, wi * 256:(wi + 1) * 256]
                     .rearrange("p (lc c) -> p lc c", lc=2),
                     wh[:, g * 128:(g + 1) * 128]
                     .rearrange("(lc p) c -> p lc c", p=128),
                     sem="bq" + "AB"[half], k=16)
            wq_dep = ("bq" + "AB"[half], p.cnt["bq" + "AB"[half]])

        csq = self.csPool[0:1, 0:1024]
        csk = self.csPool[32:33, 0:1024]
        csv = self.csPool[64:65, 0:1024]
        evac_dep = {}
        for tq in range(NT8):
            tqs = slice(tq * 512, (tq + 1) * 512)
            if a:
                mu_row = ws.mu_bro[0:1, tqs]
                mu_row32 = ws.mu_bro[32:33, tqs]
                mu_row64 = ws.mu_bro[64:65, tqs]
                Rmul = ws.R_bro[:, tqs]
            else:
                sdep = self.stat_bcast_tq(ws, tq)
                mu_row = ws.murow[0:1, :]
                mu_row32 = ws.murow[32:33, :]
                Rmul = ws.Rb_sb[:, :]
                p.wait("tensor", "DVE", sdep)
            p.waitd("vector", getattr(self, "xbf_free", None))
            p.drain("vector")
            for dc in range(NDC):
                p.op("vector", "tensor_copy", xbf[:, dc * 512:(dc + 1) * 512],
                     self.st(dc, tq * 512, 512),
                     sem="DVE" if dc == NDC - 1 else None)
            xcast = p.cnt["DVE"]
            if tq > 0:
                p.wait("tensor", "DVE", evac_dep.get(tq - 1, 0))
            p.wait("tensor", "DVE", xcast)
            if tq == 0:
                p.waitd("tensor", wq_dep)
            for dc in range(NDC):
                xs = xbf[:, dc * 512:(dc + 1) * 512]
                if a:
                    p.op("tensor", "matmul", self.bank(0),
                         wpg[:, dc * 128:(dc + 1) * 128], xs,
                         start=(dc == 0), stop=False)
                    p.op("tensor", "matmul", self.bank(1),
                         wpg[:, 1024 + dc * 128: 1024 + dc * 128 + 128], xs,
                         start=(dc == 0), stop=False)
                    for tt in range(4):
                        p.op("tensor", "matmul", self.bank(2 + tt, 128),
                             xs[:, tt * 128:(tt + 1) * 128],
                             wpg[:, 2048 + dc * 128: 2048 + dc * 128 + 128],
                             start=(dc == 0), stop=False)
                else:
                    p.op("tensor", "matmul", self.bank(0),
                         wpg[:, (g % 2) * 1024 + dc * 128:(g % 2) * 1024 + dc * 128 + 128],
                         xs, start=(dc == 0), stop=False)
            if a:
                p.op("tensor", "matmul", self.bank(0),
                     csq[:, g * 128:(g + 1) * 128], mu_row,
                     start=False, stop=True)
                p.op("tensor", "matmul", self.bank(1),
                     csk[:, g * 128:(g + 1) * 128], mu_row32,
                     start=False, stop=True)
                for tt in range(4):
                    p.op("tensor", "matmul", self.bank(2 + tt, 128),
                         mu_row64[:, tt * 128:(tt + 1) * 128],
                         csv[:, g * 128:(g + 1) * 128],
                         start=False, stop=True,
                         sem="PE" if tt == 3 else None)
            else:
                p.op("tensor", "matmul", self.bank(0),
                     csq[:, g * 128:(g + 1) * 128], mu_row,
                     start=False, stop=True)
                for lc in range(2):
                    p.op("tensor", "matmul", self.bank(1),
                         wupg[:, lc * 128: lc * 128 + 128],
                         self.lat[:, lc * T + tq * 512: lc * T + tq * 512 + 512],
                         start=(lc == 0), stop=(lc == 1))
                for tt in range(4):
                    for lc in range(2):
                        p.op("tensor", "matmul", self.bank(2 + tt, 128),
                             self.lat[:, lc * T + tq * 512 + tt * 128:
                                      lc * T + tq * 512 + tt * 128 + 128],
                             wupg[:, 256 + lc * 128: 256 + lc * 128 + 128],
                             start=(lc == 0), stop=(lc == 1),
                             sem="PE" if tt == 3 else None)
            mm_all = p.cnt["PE"]
            self.xbf_free = ("PE", mm_all)
            p.wait("vector", "PE", mm_all)
            if a:
                p.wait("vector", "ACT", getattr(self, "qsc_act_free", 0))
                p.drain("vector")
                v = p.op("vector", "tensor_tensor", qsc, self.bank(0),
                         Rmul, op=ALU.mult, sem="DVE")
                p.wait("scalar", "DVE", v)
                p.drain("scalar")
                aq = p.op("scalar", "activation", Qg[:, tqs], qsc, AF.Identity,
                          bias=self.bcol("bq", g), sem="ACT")
                p.wait("vector", "ACT", aq)
                p.drain("vector")
                v = p.op("vector", "tensor_tensor", qsc, self.bank(1),
                         Rmul, op=ALU.mult, sem="DVE")
                p.wait("scalar", "DVE", v)
                p.drain("scalar")
                ak = p.op("scalar", "activation", Kg[:, tqs], qsc, AF.Identity,
                          bias=self.bcol("bk", g), sem="ACT")
                for tt in range(4):
                    kc = tq * 4 + tt
                    p.drain("vector")
                    p.op("vector", "tensor_scalar",
                         Vb[:, kc * 130: kc * 130 + 64],
                         self.bank(2 + tt, 64),
                         self.R_tok[:, kc:kc + 1], None, op0=ALU.mult)
                    p.drain("vector")
                    p.op("vector", "tensor_scalar",
                         Vb[:, kc * 130 + 65: kc * 130 + 129],
                         self.bank(2 + tt, 64, off=64),
                         self.R_tok[:, kc:kc + 1], None, op0=ALU.mult,
                         sem="DVE" if tt == 3 else None)
                self.norms_pass(g, tq, Qg[:, tqs], "q", aq, bf=True)
                self.norms_pass(g, tq, Kg[:, tqs], "k", ak, bf=True)
                self.qsc_act_free = ak
            else:
                p.wait("vector", "ACT", getattr(self, "qsc_act_free", 0))
                p.drain("vector")
                v = p.op("vector", "tensor_tensor", qsc, self.bank(0),
                         Rmul, op=ALU.mult, sem="DVE")
                p.drain("vector")
                qpc = p.op("vector", "tensor_copy", Qg[:, tqs], qsc, sem="DVE")
                self.norms_pass(g, tq, qsc, "q", 0, bf=False, ddep=qpc)
                self.norms_pass(g, tq, self.bank(1), "k", 0, bf=False, ddep=qpc)
                self.qsc_act_free = p.cnt["ACT"]
                p.drain("vector")
                p.op("vector", "tensor_copy", Kg[:, tqs], self.bank(1), sem="DVE")
                for tt in range(4):
                    kc = tq * 4 + tt
                    p.drain("vector")
                    p.op("vector", "tensor_copy", Vb[:, kc * 130: kc * 130 + 64],
                         self.bank(2 + tt, 64))
                    p.drain("vector")
                    p.op("vector", "tensor_copy",
                         Vb[:, kc * 130 + 65: kc * 130 + 129],
                         self.bank(2 + tt, 64, off=64),
                         sem="DVE" if tt == 3 else None)
            evac_dep[tq] = p.cnt["DVE"]
        self.group_evac_done = p.cnt["DVE"]
        self.group_evac_act = p.cnt["ACT"]

    def norms_pass(self, g, tq, src, which, adep, bf, ddep=0):
        p = self.p
        sqn = self._sqn if hasattr(self, "_sqn") else None
        if which == "q":
            base = RW_QC
        else:
            base = RW_KC
        if adep:
            p.wait("scalar", "ACT", adep)
        if ddep:
            p.wait("scalar", "DVE", ddep)
        p.wait("scalar", "PE", p.cnt["PE"])
        p.drain("scalar")
        s = p.op("scalar", "activation", self.nsqn, src, AF.Square, sem="ACT")
        p.wait("tensor", "ACT", s)
        nb = 6 if self._variant_a else 7
        p.wait("tensor", "DVE", p.cnt["DVE"])
        m = p.op("tensor", "matmul", self.bank(nb, 512, p=2),
                 self.loc2[:, :], self.nsqn[:, :],
                 start=True, stop=True, sem="PE")
        p.wait("vector", "PE", m)
        c0 = base + 2 * (g % 4)
        p.drain("vector")
        p.op("vector", "tensor_reduce", self.cw(c0, 1, p=2),
             self.bank(nb, 512, p=2), axis=AX.X, op=ALU.max)
        p.drain("vector")
        if tq == 0:
            p.op("vector", "tensor_copy", self.cw(c0 + 1, 1, p=2),
                 self.cw(c0, 1, p=2), sem="DVE")
        else:
            p.op("vector", "tensor_tensor", self.cw(c0 + 1, 1, p=2),
                 self.cw(c0 + 1, 1, p=2), self.cw(c0, 1, p=2), op=ALU.max,
                 sem="DVE")
        p.drain("vector")

    def finalize_M(self, g):
        p = self.p
        cell = self.cw(RW_MC + 2 * (g % 4), 1, p=2)
        p.drain("vector")
        v = p.op("vector", "tensor_tensor", cell,
                 self.cw(RW_QC + 2 * (g % 4) + 1, 1, p=2),
                 self.cw(RW_KC + 2 * (g % 4) + 1, 1, p=2), op=ALU.mult, sem="DVE")
        p.wait("scalar", "DVE", v)
        p.drain("scalar")
        p.op("scalar", "activation", cell, cell, AF.Ln, bias=self.constA[0:2, 1:2])
        p.drain("scalar")
        s = p.op("scalar", "activation", cell, cell, AF.Exp, scale=0.5, sem="ACT")
        p.wait("vector", "ACT", s)
        p.drain("vector")
        sc = -1.0 if self._variant_a else -INV32
        v = p.op("vector", "tensor_scalar", cell, cell, sc, None, op0=ALU.mult,
                 sem="DVE")
        p.wait("tensor", "DVE", v)
        t = p.op("tensor", "transpose", self.bank(7, 2, off=256, p=1),
                 cell, self.ident_f[0:2, 0:2], sem="PE")
        p.wait("vector", "PE", t)
        p.drain("vector")
        cv = p.op("vector", "tensor_copy", self.cw(RW_MT, 2),
                  self.bank(7, 2, off=256, p=1), sem="DVE")
        p.wait("tensor", "DVE", cv)
        for loc in range(2):
            p.op("tensor", "matmul", self.bank(7, 1, off=loc),
                 self.ones_f[0:1, :], self.cw(RW_MT + loc, 1),
                 start=True, stop=True, sem="PE" if loc == 1 else None)
        m = p.cnt["PE"]
        p.wait("vector", "PE", m)
        p.drain("vector")
        md = p.op("vector", "tensor_copy", self.Mneg[:, 2 * g:2 * g + 2],
                  self.bank(7, 2), sem="DVE")
        p.drain("vector")
        return md

    def zero_qpad(self, loc, bufs):
        p = self.p
        orl = slice(64 * (1 - loc), 64 * (1 - loc) + 64)
        p.wait("vector", "PE", self.qtw_pe)
        p.drain("vector")
        for bu in bufs:
            p.op("vector", "memset", bu[orl, 0:1024], 0.0, sem="DVE")

    def build_qw(self, loc, h, w, Qg, qtw, qhw, a):
        p = self.p
        rl = slice(64 * loc, 64 * loc + 64)
        wq = slice(w * 512, w * 512 + 512)
        b = (w % 2) * 512
        p.wait("vector", "PE", self.qtw_pe)
        if a:
            p.drain("vector")
            self.qw_dep = p.op("vector", "tensor_copy", qtw[rl, b:b + 512],
                               Qg[rl, wq], sem="DVE")
            return
        im = math.exp(-LAM * 512 * w)
        ip = math.exp(LAM * 512 * w)
        p.wait("vector", "ACT", self.decay_dep[0])
        p.drain("vector")
        p.op("vector", "scalar_tensor_tensor", qtw[rl, b:b + 512],
             self.E0mb[rl, :], im, Qg[rl, wq], op0=ALU.mult, op1=ALU.mult)
        p.drain("vector")
        self.qw_dep = p.op("vector", "scalar_tensor_tensor", qhw[rl, b:b + 512],
                           self.E0pb[rl, :], ip, Qg[rl, wq],
                           op0=ALU.mult, op1=ALU.mult, sem="DVE")
        if h == 0 and w == 0:
            self.tap("qtw0", qtw[0:64, 0:512])
            self.tap("qhw0", qhw[0:64, 0:512])
            self.tap("Qg0", Qg[0:64, 0:512])

    def flush_normalize(self, atts, attst):
        p = self.p
        st = self.defer_norm
        if st is None:
            return
        self.defer_norm = None
        h, w, ab, pv = st
        p.wait("vector", "PE", max(pv, getattr(self, "rb_mm_last", 0)))
        p.drain("vector")
        v = p.op("vector", "reciprocal", self.rw(RW_RCP, 512),
                 self.P8[64:65, ab * 512: ab * 512 + 512], sem="DVE")
        p.wait("tensor", "DVE", v)
        m = p.op("tensor", "matmul", self.P8[0:64, 6 * 512:6 * 512 + 512],
                 self.ones_f[0:1, 0:64], self.rw(RW_RCP, 512),
                 start=True, stop=True, sem="PE")
        self.rb_mm_last = m
        p.wait("scalar", "PE", pv)
        p.wait("scalar", "DVE", getattr(self, "z_last", 0))
        p.drain("scalar")
        s = p.op("scalar", "activation", atts[0:64, :],
                 self.P8[0:64, ab * 512: ab * 512 + 512], AF.Copy, sem="ACT")
        p.wait("vector", "PE", m)
        p.wait("vector", "ACT", s)
        buf = w % 2
        stg = attst[0:64, buf * 512: buf * 512 + 512]
        p.waitd("vector", self.attst_dma.get(buf))
        p.drain("vector")
        z = p.op("vector", "tensor_tensor", stg, atts[0:64, :],
                 self.P8[0:64, 6 * 512:6 * 512 + 512], op=ALU.mult, sem="DVE")
        self.z_last = z
        self.attb_free[ab] = (s, z)
        p.wait("sync", "DVE", z)
        dsv = p.op("sync", "dma_start",
                   self.attscr[h * 64:(h + 1) * 64, w * 512: w * 512 + 512],
                   attst[0:64, buf * 512: buf * 512 + 512],
                   sem="as" + "AB"[buf], k=16)
        self.attst_dma[buf] = ("as" + "AB"[buf], dsv)

    def attn_core(self, ws, g, loc, h, w, variant, Qg, Kg, Vb, PT, atts,
                  attst, qtw, qhw, Ew, mdep):
        p, T = self.p, self.T
        a = variant == "a"
        NKC = self.NKC
        rl = slice(64 * loc, 64 * loc + 64)
        wq = slice(w * 512, w * 512 + 512)
        if w == 0:
            self.zero_qpad(loc, (qtw,) if a else (qtw, qhw))
            self.build_qw(loc, h, 0, Qg, qtw, qhw, a)
        qdep = self.qw_dep

        p.wait("tensor", "ACT", self.group_evac_act)
        p.wait("tensor", "DVE", self.group_evac_done)
        NKH = NKC // 2
        b2 = (w % 2) * 512
        ab = 4 + (w % 2)
        fr = self.attb_free.get(ab)
        if fr:
            p.wait("tensor", "ACT", fr[0])
            p.wait("tensor", "DVE", fr[1])
        p.wait("tensor", "DVE", self.vb_ones)
        for hp in range(2):
            exp_cnt = {}
            for jj in range(NKH):
                kc = hp * NKH + jj
                bb = jj % 4
                if len(self.quad_hist) >= 4:
                    p.wait("tensor", "ACT", self.quad_hist[-4])
                dst = self.P8[:, bb * 512: bb * 512 + 512]
                fix = 0
                if qdep and jj == 0 and hp == 0:
                    p.wait("tensor", "DVE", qdep)
                if a:
                    m = p.op("tensor", "matmul", dst,
                             Kg[:, kc * 128:(kc + 1) * 128],
                             qtw[:, b2:b2 + 512],
                             start=True, stop=True, sem="PE")
                    if kc == NKC - 1 and w == 0:
                        p.wait("vector", "PE", m)
                        p.drain("vector")
                        sl = self.P8[64:128, bb * 512: bb * 512 + 1]
                        fix = p.op("vector", "tensor_tensor", sl, sl,
                                   self.maskc[64:128, :], op=ALU.add, sem="DVE")
                else:
                    o = (kc - 4 * w) * 128
                    if o <= -128 or o >= 512:
                        rhs = (qtw[:, b2:b2 + 512] if o <= -128
                               else qhw[:, b2:b2 + 512])
                        m = p.op("tensor", "matmul", dst,
                                 Kg[:, kc * 128:(kc + 1) * 128], rhs,
                                 start=True, stop=True, sem="PE")
                    else:
                        m = p.op("tensor", "matmul", dst,
                                 Kg[:, kc * 128:(kc + 1) * 128],
                                 qtw[:, b2:b2 + 512],
                                 start=True, stop=True, sem="PE")
                        p.wait("vector", "PE", m)
                        p.wait("vector", "DVE", self.decay_dep[1])
                        p.drain("vector")
                        fix = p.op("vector", "tensor_tensor", dst, dst,
                                   self.CM[:, 384 - o:384 - o + 512],
                                   op=ALU.mult, sem="DVE")
                        if h == 0 and w == 0 and kc == 0:
                            self.tap("Kg0", Kg[:, 0:128])
                mm = p.cnt["PE"]
                p.wait("scalar", "PE", mm)
                if fix:
                    p.wait("scalar", "DVE", fix)
                if jj == 0 and hp == 0:
                    p.wait("scalar", "DVE", mdep)
                if jj == 0:
                    p.wait("scalar", "PE", getattr(self, "pt_free", 0))
                if a:
                    p.drain("scalar")
                    e = p.op("scalar", "activation", PT[:, jj * 512:(jj + 1) * 512],
                             dst, AF.Exp, bias=self.Mneg[:, h:h + 1], sem="ACT")
                else:
                    p.wait("scalar", "DVE", self.vpm_dep)
                    vcol = (self.vpm[:, kc:kc + 1] if kc <= 4 * w + 3
                            else self.vpm[:, 32 + kc:32 + kc + 1])
                    p.drain("scalar")
                    e = p.op("scalar", "activation", PT[:, jj * 512:(jj + 1) * 512],
                             dst, AF.Exp, bias=self.Mneg[:, h:h + 1], scale=vcol,
                             sem="ACT")
                    if h == 0 and w == 0 and hp == 0 and jj == 0:
                        self.tap("exp00", PT[:, 0:512])
                        self.tap("Mneg0", self.Mneg[:, 0:1])
                exp_cnt[jj] = e
                self.quad_hist.append(e)
            for jj in range(NKH):
                kc = hp * NKH + jj
                p.wait("tensor", "ACT", exp_cnt[jj])
                p.op("tensor", "matmul", self.P8[0:65, ab * 512: ab * 512 + 512],
                     Vb[:, kc * 130 + 65 * loc: kc * 130 + 65 * loc + 65],
                     PT[:, jj * 512:(jj + 1) * 512],
                     start=(kc == 0), stop=(kc == NKC - 1),
                     sem="PE" if jj == NKH - 1 else None)
            pv = p.cnt["PE"]
            self.pt_free = pv
        prev = self.defer_norm
        self.defer_norm = (h, w, ab, pv)
        if prev is not None:
            self.defer_norm = prev
            self.flush_normalize(atts, attst)
            self.defer_norm = (h, w, ab, pv)
        if w + 1 < self.NW:
            self.build_qw(loc, h, w + 1, Qg, qtw, qhw, a)
        if w == self.NW - 1:
            self.flush_normalize(atts, attst)
        self.group_core_pe = p.cnt["PE"]
        self.group_core_dve = p.cnt["DVE"]
        self.qtw_pe = p.cnt["PE"]

    def y0_add(self, ws):
        p, NT8 = self.p, self.NT8
        with ExitStack() as ses:
            tA = self.sb("y0_tA", [128, 512], F32, ses)
            for t8 in range(NT8):
                bc = self.bcast_cols(self.r1_tok[:, t8 * 4:(t8 + 1) * 4], 4, 6)
                p.wait("vector", "PE", bc)
                for dc in range(NDC):
                    xs = self.st(dc, t8 * 512, 512)
                    p.drain("vector")
                    p.op("vector", "tensor_tensor", tA, xs,
                         ws.mu_bro[:, t8 * 512:(t8 + 1) * 512], op=ALU.add)
                    p.drain("vector")
                    p.op("vector", "tensor_tensor", tA, tA, self.bank(6),
                         op=ALU.mult)
                    p.drain("vector")
                    p.op("vector", "tensor_tensor", xs, xs, tA, op=ALU.add,
                         sem="DVE" if dc == NDC - 1 else None)
                p.wait("tensor", "DVE", p.cnt["DVE"])
            p.drain("vector")

    def wo_phase(self, ws, wo_h, with_bias):
        p, T = self.p, self.T
        half_T = T // 2
        n_t8h = half_T // 512
        wn = "woa" if with_bias else "wob"
        with ExitStack() as ses:
            attb = self.sb(wn + "_att", [128, NDC * half_T], BF16, ses)
            wob = self.sb(wn + "_w", [128, 2048], BF16, ses)
            bvb = self.sb(wn + "_bvb", [128, 8], BF16, ses)
            if with_bias:
                p.drain("vector")
                bv = p.op("vector", "tensor_copy", bvb[:, :],
                          self.biasA[:, BA["bv"]:BA["bv"] + 8], sem="DVE")
            wodb = DB(2)
            evdb = DB(2)
            jd = 0
            je = 0
            for th in range(2):
                if th == 0:
                    p.waitd("sync", self.attst_dma.get(0))
                    p.waitd("sync", self.attst_dma.get(1))
                adma = 0
                for fc in range(NDC):
                    if th > 0 and fc == 0:
                        p.wait("sync", "PE", self.wo_half_pe)
                    adma = p.op("sync", "dma_start",
                                attb[:, fc * half_T:(fc + 1) * half_T],
                                self.attscr[fc * 128:(fc + 1) * 128,
                                            th * half_T:(th + 1) * half_T],
                                sem="wat", k=16)
                for dco in range(NDC):
                    b = wodb.slot(jd)
                    p.waitd("gpsimd", wodb.reuse_dep(jd))
                    d = p.op("gpsimd", "dma_start",
                             wob[:, b * 1024:(b + 1) * 1024]
                             .rearrange("p (fc c) -> p fc c", fc=NDC),
                             wo_h[:, dco * 128:(dco + 1) * 128]
                             .rearrange("(fc p) c -> p fc c", p=128),
                             sem="wo" + "AB"[b], k=16)
                    p.wait("tensor", "wo" + "AB"[b], d)
                    p.wait("tensor", "wat", adma)
                    if with_bias and th == 0:
                        p.wait("tensor", "DVE", bv)
                        for fc in range(NDC):
                            p.op("tensor", "matmul", self.bank(2, 1),
                                 wob[:, b * 1024 + fc * 128: b * 1024 + fc * 128 + 128],
                                 bvb[:, fc:fc + 1],
                                 start=(fc == 0), stop=(fc == NDC - 1),
                                 sem="PE" if fc == NDC - 1 else None)
                        mb = p.cnt["PE"]
                        p.wait("vector", "PE", mb)
                        p.drain("vector")
                        p.op("vector", "tensor_scalar",
                             self.biasA[:, BA["bo"] + dco:BA["bo"] + dco + 1],
                             self.bank(2, 1), self.vcol("bo", dco), None,
                             op0=ALU.add, sem="DVE")
                        p.wait("tensor", "DVE", p.cnt["DVE"])
                    for t8 in range(n_t8h):
                        bw = t8 % 2
                        p.waitd("tensor", self.wo_evfree.get(bw) if hasattr(self, "wo_evfree") else None)
                        for fc in range(NDC):
                            p.op("tensor", "matmul", self.bank(bw),
                                 wob[:, b * 1024 + fc * 128: b * 1024 + fc * 128 + 128],
                                 attb[:, fc * half_T + t8 * 512: fc * half_T + t8 * 512 + 512],
                                 start=(fc == 0), stop=(fc == NDC - 1),
                                 sem="PE")
                        mm = p.cnt["PE"]
                        p.wait("vector", "PE", mm)
                        xs = self.st(dco, th * half_T + t8 * 512, 512)
                        p.drain("vector")
                        if with_bias:
                            e = p.op("vector", "scalar_tensor_tensor", xs,
                                     self.bank(bw),
                                     self.biasA[:, BA["bo"] + dco:BA["bo"] + dco + 1],
                                     xs, op0=ALU.add, op1=ALU.add, sem="DVE")
                        else:
                            e = p.op("vector", "tensor_tensor", xs, self.bank(bw),
                                     xs, op=ALU.add, sem="DVE")
                        if not hasattr(self, "wo_evfree"):
                            self.wo_evfree = {}
                        self.wo_evfree[bw] = ("DVE", e)
                        je += 1
                    wodb.set_done(jd, ("PE", p.cnt["PE"]))
                    jd += 1
                self.wo_half_pe = p.cnt["PE"]
            p.drain("vector")

    def ffn_like(self, ws, ses, kind, call=0):
        p, T = self.p, self.T
        moe = kind == "moe"
        NCH = T // 512
        NI = (IMOE if moe else IMLP) // 128
        NIG = NI // 2
        xhat = self.sb(f"f_xh{kind}{call}", [128, NDC * 512], BF16, ses)
        H = self.sb(f"f_H{kind}{call}", [128, NI * 512], BF16, ses)
        w1b = self.sb(f"f_w1{kind}{call}", [128, 4096], BF16, ses)
        w2b = self.sb(f"f_w2{kind}{call}", [128, 4096], BF16, ses)
        wdb_t = self.sb(f"f_wd{kind}{call}", [128, 4096], BF16, ses) if moe else None
        wfp = self.sb(f"f_wf{kind}{call}", [128, 2048], F32, ses) if not moe else None
        comb = self.rw(0, 64, p=128) if moe else None
        cbro = self.sb(f"f_cw{kind}{call}", [128, 512], F32, ses) if moe else None
        scrF = self.sb(f"f_sc{kind}{call}", [128, 1024 if moe else 512], F32, ses)
        gsc = scrF[:, 0:512]
        tsc = scrF[:, 512:1024] if moe else scrF[:, 0:512]
        if moe:
            xf32 = scrF[:, 0:1024]
            tmpg = self.rw(RW_TMPG, 128, p=128)
            self.gate_ws = (ws, xf32, tmpg)
        if not moe:
            self.mlp_bias_prep(ws, wfp, w1b)
            self.p.barrier()
        gdb = DB(2)
        ddb = DB(2)
        sdb = DB(2)
        jg = 0
        jd = 0
        js = 0
        prev_wd_pe = 0
        prev_chunk_pe = 0
        for ch in range(NCH):
            ts0 = ch * 512
            p.wait("tensor", "DVE", p.cnt["DVE"])
            p.wait("tensor", "DVE", self.stats_done)
            self.bcast_cols(self.mu_tok[:, ch * 4:(ch + 1) * 4], 4, 6, tb=4)
            bc = self.bcast_cols(self.R_tok[:, ch * 4:(ch + 1) * 4], 4, 7, tb=4)
            mub = self.bank(6)
            Rb = self.bank(7)
            p.wait("vector", "PE", max(prev_chunk_pe, bc))
            for dc in range(NDC):
                p.drain("vector")
                p.op("vector", "tensor_tensor", tsc, self.st(dc, ts0, 512),
                     mub, op=ALU.add)
                p.drain("vector")
                p.op("vector", "tensor_tensor",
                     xhat[:, dc * 512:(dc + 1) * 512], tsc,
                     Rb, op=ALU.mult,
                     sem="DVE" if dc == NDC - 1 else None)
            xh_done = p.cnt["DVE"]
            gate_done = self.moe_gating(ch, xhat, comb, xh_done) if moe else 0
            for e in range(NE if moe else 1):
                silu_hist = []
                for ig in range(NIG):
                    b = gdb.slot(jg)
                    gs = f"g{kind}{call}" + "AB"[b]
                    if moe:
                        p.waitd("gpsimd", gdb.reuse_dep(jg))
                        if (call, ch, e, ig) == (0, 0, 0, 0):
                            for dep in self.transcode_done:
                                p.waitd("gpsimd", dep)
                        p.op("gpsimd", "dma_start",
                             w1b[:, b * 2048:(b + 1) * 2048]
                             .rearrange("p (dc i) -> p dc i", dc=NDC),
                             self.wgB[e, :, ig * 256:(ig + 1) * 256]
                             .rearrange("(dc p) i -> p dc i", p=128),
                             sem=gs, k=16)
                        d2 = p.op("gpsimd", "dma_start",
                                  w2b[:, b * 2048:(b + 1) * 2048]
                                  .rearrange("p (dc i) -> p dc i", dc=NDC),
                                  self.wuB[e, :, ig * 256:(ig + 1) * 256]
                                  .rearrange("(dc p) i -> p dc i", p=128),
                                  sem=gs, k=16)
                        wdep = (gs, d2)
                    else:
                        p.waitd("sync", getattr(self, "mlp_fold_done", None))
                        d1 = p.op("sync", "dma_start",
                                  wfp[:, :]
                                  .rearrange("p (dc i) -> p dc i", dc=NDC),
                                  self.w_h["aW1"][:, ig * 256:(ig + 1) * 256]
                                  .rearrange("(dc p) i -> p dc i", p=128),
                                  sem=gs, k=16)
                        p.wait("vector", gs, d1)
                        p.waitd("vector", gdb.reuse_dep(jg))
                        p.drain("vector")
                        for dc in range(NDC):
                            p.op("vector", "tensor_scalar",
                                 w1b[:, b * 2048 + dc * 256: b * 2048 + dc * 256 + 256],
                                 wfp[:, dc * 256: dc * 256 + 256],
                                 self.vecs[:, VEC_COLS["lnm_s"] + dc:
                                           VEC_COLS["lnm_s"] + dc + 1],
                                 None, op0=ALU.mult,
                                 sem="DVE" if dc == NDC - 1 else None)
                        wdep = ("DVE", p.cnt["DVE"])
                        self.mlp_fold_done = wdep
                    for ic in range(2):
                        i = ig * 2 + ic
                        bG = (i % 2) * 2
                        bU = bG + 1
                        p.waitd("tensor", wdep)
                        if ig == 0 and ic == 0:
                            p.wait("tensor", "DVE", xh_done)
                            p.wait("tensor", "PE", prev_wd_pe)
                        if len(silu_hist) >= 2:
                            p.wait("tensor", "ACT", silu_hist[-2][0])
                            p.wait("tensor", "DVE", silu_hist[-2][1])
                        for dc in range(NDC):
                            co = b * 2048 + dc * 256 + ic * 128
                            p.op("tensor", "matmul", self.bank(bG),
                                 w1b[:, co:co + 128],
                                 xhat[:, dc * 512:(dc + 1) * 512],
                                 start=(dc == 0), stop=(dc == NDC - 1),
                                 sem="PE" if (not moe and dc == NDC - 1) else None)
                            if moe:
                                p.op("tensor", "matmul", self.bank(bU),
                                     w2b[:, co:co + 128],
                                     xhat[:, dc * 512:(dc + 1) * 512],
                                     start=(dc == 0), stop=(dc == NDC - 1),
                                     sem="PE" if dc == NDC - 1 else None)
                        mm = p.cnt["PE"]
                        p.wait("scalar", "PE", mm)
                        if moe:
                            if silu_hist:
                                p.wait("scalar", "DVE", silu_hist[-1][1])
                            p.drain("scalar")
                            sl = p.op("scalar", "activation", gsc, self.bank(bG),
                                      AF.Silu, sem="ACT")
                            p.wait("vector", "ACT", sl)
                            p.drain("vector")
                            hm = p.op("vector", "tensor_tensor",
                                      H[:, i * 512:(i + 1) * 512], gsc,
                                      self.bank(bU), op=ALU.mult, sem="DVE")
                            silu_hist.append((sl, hm))
                        else:
                            p.drain("scalar")
                            sl = p.op("scalar", "activation",
                                      H[:, i * 512:(i + 1) * 512], self.bank(bG),
                                      AF.Gelu_apprx_tanh,
                                      bias=self.bcol("b1", i), sem="ACT")
                            silu_hist.append((sl, 0))
                    gdb.set_done(jg, ("PE", p.fence("tensor") if jg % 8 == 7
                                      else p.cnt["PE"]))
                    jg += 1
                h_act = p.cnt["ACT"]
                h_dve = p.cnt["DVE"]
                cb_done = self.moe_cbro(ch, e, comb, cbro, gate_done) if moe else 0
                for dco in range(NDC):
                    bW = 4 + (dco % 2)
                    p.waitd("tensor", sdb.reuse_dep(js))
                    if moe and dco < 2:
                        p.wait("tensor", "DVE", cb_done)
                    for ihalf in range(2):
                        b = ddb.slot(jd)
                        nic = NI // 2
                        ds2 = f"d{kind}{call}" + "AB"[b]
                        if moe:
                            p.waitd("gpsimd", ddb.reuse_dep(jd))
                            d = p.op("gpsimd", "dma_start",
                                     wdb_t[:, b * 2048:(b + 1) * 2048]
                                     .rearrange("p (r c) -> p r c", r=16),
                                     self.wdB[e, ihalf * (NI * 64):(ihalf * (NI * 64)) + NI * 64,
                                              dco * 128:(dco + 1) * 128]
                                     .rearrange("(r p) c -> p r c", p=128),
                                     sem=ds2, k=16)
                            wdep2 = (ds2, d)
                            wsrc = wdb_t
                        else:
                            p.waitd("gpsimd", ddb.reuse_dep(jd))
                            d = p.op("gpsimd", "dma_start",
                                     w2b[:, b * 2048: b * 2048 + nic * 128]
                                     .rearrange("p (r c) -> p r c", r=nic),
                                     self.w_h["aW2"][ihalf * (NI * 64):(ihalf * (NI * 64)) + NI * 64,
                                                     dco * 128:(dco + 1) * 128]
                                     .rearrange("(r p) c -> p r c", p=128),
                                     sem=ds2, k=16)
                            wdep2 = (ds2, d)
                            wsrc = w2b
                        p.waitd("tensor", wdep2)
                        if dco == 0 and ihalf == 0:
                            p.wait("tensor", "ACT", h_act)
                            p.wait("tensor", "DVE", h_dve)
                        for r in range(nic):
                            i = ihalf * nic + r
                            p.op("tensor", "matmul", self.bank(bW),
                                 wsrc[:, b * 2048 + r * 128: b * 2048 + r * 128 + 128],
                                 H[:, i * 512:(i + 1) * 512],
                                 start=(ihalf == 0 and r == 0),
                                 stop=(ihalf == 1 and r == nic - 1),
                                 sem="PE" if r == nic - 1 else None)
                        ddb.set_done(jd, ("PE", p.fence("tensor") if jd % 8 == 7
                                          else p.cnt["PE"]))
                        jd += 1
                    mm = p.cnt["PE"]
                    p.wait("vector", "PE", mm)
                    if moe:
                        p.wait("vector", "DVE", cb_done)
                    xs = self.st(dco, ch * 512, 512)
                    p.drain("vector")
                    if moe:
                        p.op("vector", "tensor_tensor", tsc, self.bank(bW),
                             cbro[:, :], op=ALU.mult)
                        p.drain("vector")
                        ev = p.op("vector", "tensor_tensor", xs, xs, tsc,
                                  op=ALU.add, sem="DVE")
                    else:
                        ev = p.op("vector", "scalar_tensor_tensor", xs,
                                  self.bank(bW), self.vcol("b2", dco), xs,
                                  op0=ALU.add, op1=ALU.add, sem="DVE")
                    sdb.set_done(js, ("DVE", ev))
                    js += 1
                prev_wd_pe = p.cnt["PE"]
            prev_chunk_pe = p.cnt["PE"]
        p.drain("vector")
        p.drain("scalar")

    def mlp_bias_prep(self, ws, wfp, w2b):
        p = self.p
        with ExitStack() as ses:
            bcolb = self.sb("m_bcol", [128, 8], BF16, ses)
            p.drain("vector")
            cb = p.op("vector", "tensor_copy", bcolb[:, :],
                      self.vecs[:, VEC_COLS["lnm_b"]:VEC_COLS["lnm_b"] + 8],
                      sem="DVE")
            wdbl = DB(2)
            jj = 0
            for rnd in range(2):
                for dc in range(NDC):
                    b = wdbl.slot(jj)
                    p.waitd("sync", wdbl.reuse_dep(jj))
                    d = p.op("sync", "dma_start", wfp[:, b * 1024: b * 1024 + 1024],
                             self.w_h["aW1"][dc * 128:(dc + 1) * 128,
                                             rnd * 1024:(rnd + 1) * 1024],
                             sem="mb" + "AB"[b], k=16)
                    p.wait("vector", "mb" + "AB"[b], d)
                    p.drain("vector")
                    p.waitd("vector", getattr(self, "mlpb_free", None))
                    c = p.op("vector", "tensor_copy",
                             w2b[:, b * 1024: b * 1024 + 1024],
                             wfp[:, b * 1024: b * 1024 + 1024], sem="DVE")
                    wdbl.set_done(jj, ("DVE", c))
                    p.wait("tensor", "DVE", max(c, cb))
                    for oc in range(NDC):
                        p.op("tensor", "matmul", self.bank(oc, 1),
                             w2b[:, b * 1024 + oc * 128: b * 1024 + oc * 128 + 128],
                             bcolb[:, dc:dc + 1],
                             start=(dc == 0), stop=(dc == NDC - 1),
                             sem="PE" if oc == NDC - 1 else None)
                    if jj >= 1:
                        self.mlpb_free = ("PE", p.cnt["PE"])
                    jj += 1
                done = p.cnt["PE"]
                p.wait("vector", "PE", done)
                for oc in range(NDC):
                    i16 = rnd * 8 + oc
                    p.drain("vector")
                    p.op("vector", "tensor_scalar",
                         self.biasA[:, BA["b1"] + i16:BA["b1"] + i16 + 1],
                         self.bank(oc, 1), self.vcol("b1", i16), None,
                         op0=ALU.add, sem="DVE")
            p.drain("vector")

    def moe_gating(self, ch, xhat, comb, xh_done):
        p = self.p
        ws, xf32, tmpg = self.gate_ws
        mub = self.bank(6)
        Rb = self.bank(7)
        p.wait("tensor", "DVE", max(xh_done, self.gate_dep))
        L = self.rw(RW_GT, 8, p=128)
        m1 = self.rw(RW_GT + 8, 1, p=128)
        m2 = self.rw(RW_GT + 9, 1, p=128)
        e1 = self.rw(RW_GT + 16, 8, p=128)
        L2 = self.rw(RW_GT + 24, 8, p=128)
        p12 = self.rw(RW_GT + 32, 2, p=128)
        ts0 = ch * 512
        for tt in range(4):
            p.wait("vector", "PE", p.cnt["PE"])
            for dc in range(NDC):
                sl = self.st(dc, ts0 + tt * 128, 128)
                mus = mub[:, tt * 128: tt * 128 + 128]
                rs = Rb[:, tt * 128: tt * 128 + 128]
                p.drain("vector")
                p.op("vector", "tensor_tensor", tmpg, sl, mus, op=ALU.add)
                p.drain("vector")
                p.op("vector", "tensor_tensor",
                     xf32[:, dc * 128:(dc + 1) * 128], tmpg, rs, op=ALU.mult,
                     sem="DVE" if dc == NDC - 1 else None)
            xfd = p.cnt["DVE"]
            p.wait("tensor", "DVE", xfd)
            for dc in range(NDC):
                p.op("tensor", "matmul", self.bank(5, 8),
                     xf32[:, dc * 128:(dc + 1) * 128],
                     self.vecs[:, VEC_COLS["gate"] + dc * 8:
                               VEC_COLS["gate"] + dc * 8 + 8],
                     start=(dc == 0), stop=(dc == NDC - 1),
                     sem="PE" if dc == NDC - 1 else None)
            mm = p.cnt["PE"]
            p.wait("vector", "PE", mm)
            p.drain("vector")
            p.op("vector", "tensor_copy", L, self.bank(5, 8))
            p.drain("vector")
            lv = p.op("vector", "tensor_copy", L2, self.bank(5, 8), sem="DVE")
            p.drain("vector")
            p.op("vector", "tensor_reduce", m1, L, axis=AX.X, op=ALU.max)
            p.drain("vector")
            p.op("vector", "tensor_scalar", e1, L, m1, None, op0=ALU.is_equal)
            p.drain("vector")
            p.op("vector", "scalar_tensor_tensor", L2, e1, -BIG, L,
                 op0=ALU.mult, op1=ALU.add)
            p.drain("vector")
            p.op("vector", "tensor_reduce", m2, L2, axis=AX.X, op=ALU.max)
            p.drain("vector")
            p.op("vector", "tensor_scalar", p12[:, 0:1], m1, m2, None,
                 op0=ALU.subtract)
            p.drain("vector")
            v = p.op("vector", "tensor_scalar", p12[:, 1:2], m2, m1, None,
                     op0=ALU.subtract, sem="DVE")
            p.wait("scalar", "DVE", v)
            p.drain("scalar")
            s = p.op("scalar", "activation", p12, p12, AF.Sigmoid, sem="ACT")
            p.wait("vector", "ACT", s)
            p.drain("vector")
            p.op("vector", "tensor_scalar", L2, L2, m2, None, op0=ALU.is_equal)
            p.drain("vector")
            p.op("vector", "tensor_scalar", e1, e1, p12[:, 0:1], None,
                 op0=ALU.mult)
            p.drain("vector")
            p.op("vector", "tensor_scalar", L2, L2, p12[:, 1:2], None,
                 op0=ALU.mult)
            p.drain("vector")
            p.op("vector", "tensor_tensor", comb[:, tt * 8:(tt + 1) * 8],
                 e1, L2, op=ALU.add, sem="DVE")
            p.drain("vector")
        return p.cnt["DVE"]

    def moe_cbro(self, ch, e, comb, cbro, gate_done):
        p = self.p
        p.wait("tensor", "DVE", gate_done)
        if hasattr(self, "cbro_last"):
            p.wait("tensor", "DVE", self.cbro_last)
        p.wait("tensor", "DVE", p.cnt["DVE"])
        for tt in range(4):
            m = p.op("tensor", "transpose", self.bank(4, 128, p=1),
                     comb[:, tt * 8 + e: tt * 8 + e + 1], self.ident_f[:, :],
                     sem="PE")
            p.wait("vector", "PE", m)
            p.drain("vector")
            v = p.op("vector", "tensor_copy", self.rw(RW_BCA + tt * 128, 128),
                     self.bank(4, 128, p=1), sem="DVE")
            p.wait("tensor", "DVE", v)
            p.op("tensor", "matmul", self.bank(5, 128, off=tt * 128),
                 self.ones_f[0:1, :], self.rw(RW_BCA + tt * 128, 128),
                 start=True, stop=True, sem="PE")
        bc = p.cnt["PE"]
        p.wait("vector", "PE", bc)
        p.drain("vector")
        v = p.op("vector", "tensor_copy", cbro[:, :], self.bank(5), sem="DVE")
        p.drain("vector")
        self.cbro_last = v
        return v

    def sub_moe(self, call):
        p, T = self.p, self.T
        with ExitStack() as pes:
            ws = WS()
            with ExitStack() as ses:
                ws.sq = self.sb(f"mo{call}_sq", [128, 1024], F32, ses)
                ws.stat = self.sb(f"mo{call}_stat", [128, 2048], F32, ses)
                self.ln_stats(ws, double=False, bro=False)
            self.p.barrier()
            with ExitStack() as ses:
                self.ffn_like(ws, ses, kind="moe", call=call)
            self.p.barrier()

    def sub_attn_b(self):
        p, T = self.p, self.T
        self._variant_a = False
        with ExitStack() as pes:
            ws = WS()
            with ExitStack() as ses:
                ws.sq = self.sb("b_sq", [128, 1024], F32, ses)
                ws.stat = self.sb("b_stat", [128, 2048], F32, ses)
                self.ln_stats(ws, double=False, bro=False)
            self.p.barrier()
            self.lat = self.sb("b_lat", [128, 2 * T], BF16, pes)
            ws.murow = self.sb("b_murow", [64, 512], BF16, pes)
            ws.Rb_sb = self.sb("b_Rb", [128, 512], F32, pes)
            with ExitStack() as ses:
                ws.wfp = self.sb("b_wfp", [128, 2048], F32, ses)
                ws.wca = self.sb("b_wca", [128, 2048], BF16, ses)
                ws.rhs2 = self.sb("b_rhs2", [128, 16], F32, ses)
                ws.rhs2b = self.sb("b_rhs2b", [128, 16], BF16, ses)
                ws.cscol = self.sb("b_cscol", [128, 1], F32, ses)
                self.prep_bias_cs(self.w_h["bWq"], "lnq_s", "lnq_b", "bq", "bq",
                                  (0, 0), 1.0, ws, want_bias=False,
                                  identity_s=True)
            self.p.barrier()
            with ExitStack() as ses:
                ws.wfp = self.sb("b_wfp2", [128, 2048], F32, ses)
                ws.cscol = self.sb("b_cscol2", [128, 1], F32, ses)
                self.kv_latent(ws, ses)
            self.p.barrier()
            self.attn_groups(ws, variant="b")
            self.p.barrier()
            self.wo_phase(ws, self.w_h["bWo"], with_bias=False)
            self.p.barrier()
        self._variant_a = True

    def kv_latent(self, ws, ses):
        p, T, NT8 = self.p, self.T, self.NT8
        wkvb = self.sb("b_wkv", [128, 8192], BF16, ses)
        wdwn = self.sb("b_wdwn", [128, 2048], BF16, ses)
        kvb = self.sb("b_kvb", [128, 1024], BF16, ses)
        xb2 = self.sb("b_xb2", [128, 4096], BF16, ses)
        cskv = self.csPool[32:33, 0:1024]
        p.op("gpsimd", "dma_start",
             wdwn[:, :].rearrange("p (dc l) -> p dc l", dc=NDC),
             self.w_h["bWdown"][:, :].rearrange("(dc p) l -> p dc l", p=128),
             sem="bup", k=16)
        dw = p.cnt["bup"]
        csacc = DB(1)
        for dcin in range(NDC):
            p.waitd("sync", csacc.reuse_dep(dcin))
            kvs = "kwA"
            p.op("sync", "dma_start", ws.wfp[:, 0:1024],
                 self.w_h["bWk"][dcin * 128:(dcin + 1) * 128, :], sem=kvs, k=16)
            d2 = p.op("sync", "dma_start", ws.wfp[:, 1024:2048],
                      self.w_h["bWv"][dcin * 128:(dcin + 1) * 128, :],
                      sem=kvs, k=16)
            p.wait("vector", kvs, d2)
            p.drain("vector")
            c = p.op("vector", "tensor_tensor",
                     wkvb[:, dcin * 1024:(dcin + 1) * 1024],
                     ws.wfp[:, 0:1024], ws.wfp[:, 1024:2048], op=ALU.add,
                     sem="DVE")
            csacc.set_done(dcin, ("DVE", c))
            p.wait("tensor", "DVE", c)
            for oc in range(NDC):
                p.op("tensor", "matmul", self.bank(oc, 1),
                     wkvb[:, dcin * 1024 + oc * 128: dcin * 1024 + oc * 128 + 128],
                     self.ones_b[:, 0:1], start=(dcin == 0), stop=(dcin == NDC - 1),
                     sem="PE" if oc == NDC - 1 else None)
        done = p.cnt["PE"]
        p.wait("vector", "PE", done)
        for oc in range(NDC):
            p.drain("vector")
            v = p.op("vector", "tensor_copy", ws.cscol[:, 0:1], self.bank(oc, 1),
                     sem="DVE")
            p.wait("tensor", "DVE", v)
            m = p.op("tensor", "transpose", self.bank(oc, 128, off=256, p=1),
                     ws.cscol[:, 0:1], self.ident_f[:, :], sem="PE")
            p.wait("vector", "PE", m)
            p.drain("vector")
            p.op("vector", "tensor_copy",
                 cskv[0:1, oc * 128: oc * 128 + 128],
                 self.bank(oc, 128, off=256, p=1), sem="DVE")
        cs_done = p.cnt["DVE"]
        p.wait("tensor", "bup", dw)
        kdb = DB(2)
        lat_ev = {}
        jk = 0
        for tq in range(NT8):
            tqs = slice(tq * 512, (tq + 1) * 512)
            sdep = self.stat_bcast_tq(ws, tq)
            p.wait("tensor", "DVE", sdep)
            if tq >= 1:
                p.wait("vector", "PE", self.kvx_pe.get(tq - 1, 0) if hasattr(self, "kvx_pe") else 0)
            p.drain("vector")
            for dc in range(NDC):
                p.op("vector", "tensor_copy", xb2[:, dc * 512:(dc + 1) * 512],
                     self.st(dc, tq * 512, 512),
                     sem="DVE" if dc == NDC - 1 else None)
            xc = p.cnt["DVE"]
            p.wait("tensor", "DVE", max(xc, cs_done))
            if tq >= 1:
                p.wait("tensor", "DVE", lat_ev.get(tq - 1, 0))
            for dcout in range(NDC):
                bk = dcout % 2
                for dcin in range(NDC):
                    p.op("tensor", "matmul", self.bank(bk),
                         wkvb[:, dcin * 1024 + dcout * 128: dcin * 1024 + dcout * 128 + 128],
                         xb2[:, dcin * 512:(dcin + 1) * 512],
                         start=(dcin == 0), stop=False)
                m = p.op("tensor", "matmul", self.bank(bk),
                         cskv[0:1, dcout * 128: dcout * 128 + 128],
                         ws.murow[32:33, :],
                         start=False, stop=True, sem="PE")
                p.wait("vector", "PE", m)
                bkv = kdb.slot(jk)
                p.waitd("vector", kdb.reuse_dep(jk))
                p.drain("vector")
                kv = p.op("vector", "tensor_tensor",
                          kvb[:, bkv * 512:(bkv + 1) * 512], self.bank(bk),
                          ws.Rb_sb[:, :], op=ALU.mult, sem="DVE")
                p.wait("tensor", "DVE", kv)
                for lc in range(2):
                    mm = p.op("tensor", "matmul", self.bank(2 + lc),
                              wdwn[:, dcout * 256 + lc * 128: dcout * 256 + lc * 128 + 128],
                              kvb[:, bkv * 512:(bkv + 1) * 512],
                              start=(dcout == 0), stop=(dcout == NDC - 1),
                              sem="PE")
                kdb.set_done(jk, ("PE", p.cnt["PE"]))
                jk += 1
            lm = p.cnt["PE"]
            p.wait("vector", "PE", lm)
            for lc in range(2):
                p.drain("vector")
                p.op("vector", "tensor_copy",
                     self.lat[:, lc * T + tq * 512: lc * T + tq * 512 + 512],
                     self.bank(2 + lc),
                     sem="DVE" if lc == 1 else None)
            lat_ev[tq] = p.cnt["DVE"]
            if not hasattr(self, "kvx_pe"):
                self.kvx_pe = {}
            self.kvx_pe[tq] = lm
        p.drain("vector")
        self.lat_done = p.cnt["DVE"]
        p.wait("tensor", "DVE", self.lat_done)

    def sub_attn_a_entry(self):
        pass



def _prep_inputs(inputs):
    g = lambda k: np.asarray(inputs[k])

    def vec(v):
        v = np.asarray(v)
        return np.ascontiguousarray(v.reshape(-1, 128).T.astype(np.float32))

    vecs = np.zeros((128, NVEC), dtype=np.float32)

    def put(name, v):
        c = VEC_COLS[name]
        vv = vec(v)
        vecs[:, c:c + vv.shape[1]] = vv

    put("lnq_s", g("L0_attn_a_lnq_s")); put("lnq_b", g("L0_attn_a_lnq_b"))
    put("lnk_s", g("L0_attn_a_lnk_s")); put("lnk_b", g("L0_attn_a_lnk_b"))
    put("lnv_s", g("L0_attn_a_lnv_s")); put("lnv_b", g("L0_attn_a_lnv_b"))
    put("lnm_s", g("L0_attn_a_lnm_s")); put("lnm_b", g("L0_attn_a_lnm_b"))
    put("bq", g("L0_attn_a_bq")); put("bk", g("L0_attn_a_bk"))
    put("bv", g("L0_attn_a_bv")); put("bo", g("L0_attn_a_bo"))
    put("b1", g("L0_attn_a_b1")); put("b2", g("L0_attn_a_b2"))
    gw = g("L0_ffn_gate_w").T.reshape(NDC, 128, NE).transpose(1, 0, 2).reshape(128, 64)
    vecs[:, VEC_COLS["gate"]:VEC_COLS["gate"] + 64] = gw

    C = np.ascontiguousarray
    f4 = np.float32
    common = {
        "vecsin": vecs,
        "aWq": C(g("L0_attn_a_Wq").astype(f4)),
        "aWk": C(g("L0_attn_a_Wk").astype(f4)),
        "aWv": C(g("L0_attn_a_Wv").astype(f4)),
        "aWo": C(g("L0_attn_a_Wo").astype(f4)),
        "aW1": C(g("L0_attn_a_W1").astype(f4)),
        "aW2": C(g("L0_attn_a_W2").astype(f4)),
        "bWq": C(g("L0_attn_b_W_q").astype(f4)),
        "bWk": C(g("L0_attn_b_W_k").astype(f4)),
        "bWv": C(g("L0_attn_b_W_v").astype(f4)),
        "bWo": C(g("L0_attn_b_W_o").astype(f4)),
        "bWdown": C(g("L0_attn_b_W_down_kv").astype(f4)),
        "bWupK": C(g("L0_attn_b_W_up_k").astype(f4)),
        "bWupV": C(g("L0_attn_b_W_up_v").astype(f4)),
        "wgT": C(g("L0_ffn_wg").transpose(0, 2, 1).astype(f4)),
        "wuT": C(g("L0_ffn_wu").transpose(0, 2, 1).astype(f4)),
        "wdT": C(g("L0_ffn_wd").transpose(0, 2, 1).astype(f4)),
    }
    x = np.asarray(inputs["L0_attn_a_x"])
    xs = [C(x[b].T.astype(f4)) for b in range(x.shape[0])]
    return common, xs


MK._variant_a = True


def kernel(**inputs):
    common, xs = _prep_inputs(inputs)
    mk = MK(T=4096)
    in_maps = [dict(common, x=xs[b]) for b in range(8)]
    res = run_bass_kernel_spmd(mk.nc, in_maps, core_ids=list(range(8)))
    outs = [np.asarray(r["out"]) for r in res.results]
    y = np.stack([o.T for o in outs], axis=0)
    return np.ascontiguousarray(y.astype(np.float32))
